# version 14
# baseline (speedup 1.0000x reference)
import contextlib
import numpy as np
import concourse.bass as bass
import concourse.mybir as mybir
import concourse.bass_utils as _bu
from concourse.bass_utils import run_bass_kernel_spmd

F32 = mybir.dt.float32
BF16 = mybir.dt.bfloat16
AF = mybir.ActivationFunctionType
OP = mybir.AluOpType
AX = mybir.AxisListType

P = 128
T = 1024
H = 4096
HC = 32
D = 1024
DC = 8
I = 14336
IC = 112
EPS = 1e-5
NCORES = 8
SB_BASE = 18432
SBUF_CAP = 229344


class Sem:
    def __init__(self, hdl):
        self.h = hdl
        self.n = 0

    def inc(self, k=1):
        self.n += k
        return self.n


class Env:
    pass


def build(dump=None, detect_races=False):
    nc = bass.Bass("TRN2", target_bir_lowering=False,
                   detect_race_conditions=detect_races)

    d = {}
    d["x"] = nc.dram_tensor("x", [H, T], F32, kind="ExternalInput").ap()
    d["w1"] = nc.dram_tensor("w1t", [IC, P, HC, P], F32, kind="ExternalInput").ap()
    d["w3"] = nc.dram_tensor("w3t", [IC, P, HC, P], F32, kind="ExternalInput").ap()
    d["w2"] = nc.dram_tensor("w2t", [4, IC // 4, P, 4, 1024], F32, kind="ExternalInput").ap()
    d["s1"] = nc.dram_tensor("s1r", [P, IC * HC], F32, kind="ExternalInput").ap()
    d["s3"] = nc.dram_tensor("s3r", [P, IC * HC], F32, kind="ExternalInput").ap()
    d["s2"] = nc.dram_tensor("s2r", [P, IC * HC], F32, kind="ExternalInput").ap()
    d["wqk"] = nc.dram_tensor("wqkv_qk", [16, P, HC, P], F32, kind="ExternalInput").ap()
    d["wv"] = nc.dram_tensor("wqkv_v", [2, P, HC, 512], F32, kind="ExternalInput").ap()
    d["bq"] = nc.dram_tensor("bq", [P, DC], F32, kind="ExternalInput").ap()
    d["bk"] = nc.dram_tensor("bk", [P, DC], F32, kind="ExternalInput").ap()
    d["bv"] = nc.dram_tensor("bvr", [P, D], F32, kind="ExternalInput").ap()
    d["wo"] = nc.dram_tensor("wo", [HC, P, DC, P], F32, kind="ExternalInput").ap()
    d["bo"] = nc.dram_tensor("bo", [P, HC], F32, kind="ExternalInput").ap()
    d["ident"] = nc.dram_tensor("ident", [P, P], F32, kind="ExternalInput").ap()
    d["out"] = nc.dram_tensor("out", [H, T], F32, kind="ExternalOutput").ap()
    dbg_shape = [I, T] if dump == "h" else [H, T]
    d["dbg"] = (nc.dram_tensor("dbg", dbg_shape, F32, kind="ExternalOutput").ap()
                if dump is not None else None)
    if dump == "attn":
        d["dbg_k"] = nc.dram_tensor("dbg_k", [4 * D, T], F32, kind="ExternalOutput").ap()
        d["dbg_v"] = nc.dram_tensor("dbg_v", [4 * T, D], F32, kind="ExternalOutput").ap()
        d["dbg_q"] = nc.dram_tensor("dbg_q", [P, DC * T], F32, kind="ExternalOutput").ap()
        d["dbg_o"] = nc.dram_tensor("dbg_o", [P, DC * T], F32, kind="ExternalOutput").ap()
    if dump == "attn2":
        d["dbg_p"] = nc.dram_tensor("dbg_p", [P, 4 * T], F32, kind="ExternalOutput").ap()
        d["dbg_l"] = nc.dram_tensor("dbg_l", [P, 4], F32, kind="ExternalOutput").ap()
        d["dbg_ot"] = nc.dram_tensor("dbg_ot", [P, D], F32, kind="ExternalOutput").ap()
        d["dbg_o"] = nc.dram_tensor("dbg_o", [P, DC * T], F32, kind="ExternalOutput").ap()
        d["dbg_s"] = nc.dram_tensor("dbg_s", [P, 4 * T], F32, kind="ExternalOutput").ap()

    d["xr0"] = nc.dram_tensor("xr0", [H, T], F32, kind="Internal").ap()
    d["xr1"] = nc.dram_tensor("xr1", [H, T], F32, kind="Internal").ap()
    d["hst"] = nc.dram_tensor("hst", [I, T], BF16, kind="Internal").ap()
    d["w2dq"] = nc.dram_tensor("w2dq", [4, IC // 4, P, 4096], BF16, kind="Internal").ap()
    d["kin"] = nc.dram_tensor("kin", [D, T], BF16, kind="Internal").ap()
    d["kga"] = nc.dram_tensor("kga", [4 * D, T], BF16, kind="Internal").ap()
    d["vin"] = nc.dram_tensor("vin", [T, D], BF16, kind="Internal").ap()
    d["vga"] = nc.dram_tensor("vga", [4 * T, D], BF16, kind="Internal").ap()

    off = {}
    cur = SB_BASE

    def place(name, nbytes, align=32):
        nonlocal cur
        cur = (cur + align - 1) // align * align
        off[name] = cur
        cur += nbytes

    place("s1r", HC * IC * 2)
    place("s3r", HC * IC * 2)
    place("s2r", HC * IC * 2)
    place("bq", DC * 4)
    place("bk", DC * 4)
    place("bo", HC * 4)
    place("bvr", D * 4)
    place("ident", P * 2)
    place("ones_bf", 2)
    place("ones_row", P * 4)
    place("m_row", T * 4)
    place("r_row", T * 4)
    place("mr_row", T * 4)
    place("r_bc", T * 4)
    place("mr_bc", T * 4)
    place("l_sb", 4)
    place("rl_sb", 4)
    place("uf", 4 * T * 4)
    place("ub", 4 * T * 2)
    place("BIG", 0)
    BIG = off["BIG"]
    assert BIG + 135168 <= SBUF_CAP, (BIG,)

    lay = [
        ("s1r", [P, IC * HC], BF16, off["s1r"]),
        ("s3r", [P, IC * HC], BF16, off["s3r"]),
        ("s2r", [P, IC * HC], BF16, off["s2r"]),
        ("bq", [P, DC], F32, off["bq"]),
        ("bk", [P, DC], F32, off["bk"]),
        ("bo", [P, HC], F32, off["bo"]),
        ("bvr", [P, D], F32, off["bvr"]),
        ("ident", [P, P], BF16, off["ident"]),
        ("ones_bf", [P, 1], BF16, off["ones_bf"]),
        ("ones_row", [1, P], F32, off["ones_row"]),
        ("m_row", [1, T], F32, off["m_row"]),
        ("r_row", [1, T], F32, off["r_row"]),
        ("mr_row", [1, T], F32, off["mr_row"]),
        ("r_bc", [P, T], F32, off["r_bc"]),
        ("mr_bc", [P, T], F32, off["mr_bc"]),
        ("l_sb", [P, 1], F32, off["l_sb"]),
        ("rl_sb", [P, 1], F32, off["rl_sb"]),
        ("uf", [P, 4, T], F32, off["uf"]),
        ("ub", [P, 4, T], BF16, off["ub"]),
        ("n_fm", [P, HC, T], BF16, BIG + 0),
        ("w1b", [P, 3, HC, P], BF16, BIG + 65536),
        ("w3b", [P, 3, HC, P], BF16, BIG + 90112),
        ("sg", [P, 2, T], BF16, BIG + 114688),
        ("hb", [P, 2, T], BF16, BIG + 118784),
        ("h_res", [P, IC, 512], BF16, BIG + 0),
        ("w2b", [P, 2, 4, 8, P], BF16, BIG + 114688),
        ("q_bf", [P, DC, T], BF16, BIG + 65536),
        ("wqb", [P, 2, HC, P], BF16, BIG + 81920),
        ("wvb", [P, HC, 512], BF16, BIG + 98304),
        ("kv_st", [P, 2, T], BF16, BIG + 131072),
        ("K_sb", [P, DC, 4 * T], BF16, BIG + 0),
        ("o_fm", [P, DC, T], BF16, BIG + 81920),
        ("p_bf", [P, 4 * T], BF16, BIG + 98304),
        ("pT", [P, 4, P], BF16, BIG + 106496),
        ("vb", [P, 4, D], BF16, BIG + 107520),
        ("o_tm", [P, D], BF16, BIG + 115712),
        ("wob", [P, 4, DC, P], BF16, BIG + 117760),
    ]
    S = {}
    for name, shape, dtype, o in lay:
        S[name] = nc.alloc_sbuf_tensor_at("sb_" + name, shape, dtype, offset=o).ap()

    PS = nc.alloc_psum_tensor("PS", [P, 4096], F32).ap()
    PSb = PS.bitcast(BF16)

    ctx = contextlib.ExitStack()

    def mksem(name):
        return Sem(ctx.enter_context(nc.semaphore("sm_" + name)))

    s = {}
    for name in ["dw", "dxm", "pe", "dve", "act", "pool", "gj", "cck", "ccv",
                 "WV", "XL", "KL", "HL", "DBG"]:
        s[name] = mksem(name)
    for name, k in [("LD", 4), ("WA", 3), ("HS", 2), ("W2", 4), ("W2S", 4),
                    ("C2", 4), ("XS", 2), ("QW", 2), ("KS", 2), ("VB", 4),
                    ("WO", 4)]:
        s[name] = [mksem(f"{name}{i}") for i in range(k)]

    dma_sems = ([s["dw"], s["dxm"], s["WV"], s["XL"], s["KL"], s["HL"]]
                + s["LD"] + s["WA"] + s["HS"] + s["W2"] + s["W2S"] + s["C2"]
                + s["XS"] + s["QW"] + s["KS"] + s["VB"] + s["WO"])

    blk = ctx.enter_context(nc.Block())

    e = Env()
    e.nc, e.blk, e.S, e.PS, e.PSb = nc, blk, S, PS, PSb
    e.sem = s
    e.dma_sems = dma_sems
    e.d = d
    e.njoin = 0
    _emit(e, dump)
    ctx.close()
    return nc


def _join(e, skip_pool=False):
    s = e.sem
    nparts = 4 if skip_pool else 5
    tgt = s["gj"].n + nparts

    def syncb(q):
        for ds in e.dma_sems:
            if ds.n:
                q.wait_ge(ds.h, ds.n)
        q.sem_inc(s["gj"].h, 1)
        q.wait_ge(s["gj"].h, tgt)

    def plain(q):
        q.sem_inc(s["gj"].h, 1)
        q.wait_ge(s["gj"].h, tgt)

    e.blk.sync(syncb)
    if not skip_pool:
        e.blk.gpsimd(plain)
    e.blk.scalar(plain)
    e.blk.tensor(plain)
    e.blk.vector(plain)
    s["gj"].n = tgt


def _emit(e, dump):
    d, S, s = e.d, e.S, e.sem

    def initpool(g):
        g.dma_start(out=S["ident"], in_=d["ident"]).then_inc(s["dw"].h, 16)
        g.dma_start(out=S["s1r"], in_=d["s1"]).then_inc(s["dw"].h, 16)
        g.dma_start(out=S["s3r"], in_=d["s3"]).then_inc(s["dw"].h, 16)
        g.dma_start(out=S["s2r"], in_=d["s2"]).then_inc(s["dw"].h, 16)
        s["dw"].inc(64)

    def initsync(q):
        q.dma_start(out=S["bq"], in_=d["bq"]).then_inc(s["dxm"].h, 16)
        q.dma_start(out=S["bk"], in_=d["bk"]).then_inc(s["dxm"].h, 16)
        q.dma_start(out=S["bo"], in_=d["bo"]).then_inc(s["dxm"].h, 16)
        q.dma_start(out=S["bvr"], in_=d["bv"]).then_inc(s["dxm"].h, 16)
        s["dxm"].inc(64)

    def initvec(v):
        v.memset(S["ones_bf"], 1.0)
        v.memset(S["ones_row"], 1.0).then_inc(s["dve"].h, 1)
        s["dve"].inc()

    e.blk.gpsimd(initpool)
    e.blk.sync(initsync)
    e.blk.vector(initvec)
    _ffnA_prefetch(e, 3)

    _ln(e, d["x"])
    _join(e)
    if dump == "ln":
        def dmpn(g):
            g.dma_start(out=d["dbg"].rearrange("(hc p) t -> p hc t", p=P),
                        in_=S["n_fm"]).then_inc(s["DBG"].h, 16)
            g.dma_start(out=d["out"], in_=d["x"]).then_inc(s["DBG"].h, 16)
            s["DBG"].inc(32)
            g.wait_ge(s["DBG"].h, s["DBG"].n)
        e.blk.gpsimd(dmpn)
        return
    _ffnA(e)
    _join(e)
    if dump == "h":
        def dmph(g):
            g.dma_start(out=d["dbg"], in_=d["hst"]).then_inc(s["DBG"].h, 16)
            g.dma_start(out=d["out"], in_=d["x"]).then_inc(s["DBG"].h, 16)
            s["DBG"].inc(32)
            g.wait_ge(s["DBG"].h, s["DBG"].n)
        e.blk.gpsimd(dmph)
        return
    _ffnB(e, pass1=True, xsrc=d["x"], xdst=d["xr0"])

    if dump == "x1":
        _dump(e, d["xr0"])
        return

    _ln(e, d["xr0"])
    _join(e)
    _qkv(e)
    _join(e)
    _attn_core(e, nqt=(2 if dump == "attn2" else 8))
    _join(e)
    if dump == "attn2":
        def dmpa2(g):
            XSs = s["DBG"]
            g.dma_start(out=d["dbg_p"], in_=S["p_bf"]).then_inc(XSs.h, 16)
            with e.nc.allow_non_contiguous_dma(reason="debug dump"):
                g.dma_start(out=d["dbg_l"][:, 0:1], in_=S["l_sb"]).then_inc(XSs.h, 16)
                g.dma_start(out=d["dbg_l"][:, 1:2], in_=S["rl_sb"]).then_inc(XSs.h, 16)
                g.dma_start(out=d["dbg_l"][:, 2:3], in_=S["r_bc"][:, 0:1]).then_inc(XSs.h, 16)
            g.dma_start(out=d["dbg_ot"], in_=S["o_tm"]).then_inc(XSs.h, 16)
            g.dma_start(out=d["dbg_o"],
                        in_=S["o_fm"].rearrange("p a b -> p (a b)")).then_inc(XSs.h, 16)
            g.dma_start(out=d["out"], in_=d["xr0"]).then_inc(XSs.h, 16)
            g.dma_start(out=d["dbg_s"],
                        in_=S["uf"].rearrange("p a b -> p (a b)")).then_inc(XSs.h, 16)
            g.dma_start(out=d["dbg"], in_=d["xr0"]).then_inc(XSs.h, 16)
            XSs.inc(144)
            g.wait_ge(XSs.h, XSs.n)
        e.blk.gpsimd(dmpa2)
        return
    _outproj(e, xsrc=d["xr0"], xdst=d["xr1"])
    _join(e)

    if dump == "x2":
        _dump(e, d["xr1"])
        return
    if dump == "attn":
        def dmpa(g):
            XSs = s["DBG"]
            g.dma_start(out=d["dbg"], in_=d["xr1"]).then_inc(XSs.h, 16)
            g.dma_start(out=d["out"], in_=d["xr1"]).then_inc(XSs.h, 16)
            g.dma_start(out=d["dbg_k"], in_=d["kga"]).then_inc(XSs.h, 16)
            g.dma_start(out=d["dbg_v"], in_=d["vga"]).then_inc(XSs.h, 16)
            g.dma_start(out=d["dbg_q"],
                        in_=S["q_bf"].rearrange("p a b -> p (a b)")).then_inc(XSs.h, 16)
            g.dma_start(out=d["dbg_o"],
                        in_=S["o_fm"].rearrange("p a b -> p (a b)")).then_inc(XSs.h, 16)
            XSs.inc(96)
            g.wait_ge(XSs.h, XSs.n)
        e.blk.gpsimd(dmpa)
        return

    _ln(e, d["xr1"])
    _join(e)
    _ffnA(e)
    _join(e)
    _ffnB(e, pass1=False, xsrc=d["xr1"], xdst=d["out"])
    _join(e)


def _dump(e, src):
    s = e.sem

    def dmp(q):
        q.dma_start(out=e.d["dbg"], in_=src).then_inc(s["DBG"].h, 16)
        q.dma_start(out=e.d["out"], in_=src).then_inc(s["DBG"].h, 16)
        s["DBG"].inc(32)
        q.wait_ge(s["DBG"].h, s["DBG"].n)

    e.blk.gpsimd(dmp)


def _ln(e, xsrc):
    S, s, PS = e.S, e.sem, e.PS
    uf, ub = S["uf"], S["ub"]
    LD = s["LD"]
    LB = [x.n for x in LD]
    AB = s["act"].n
    PB = s["pe"].n
    DB = s["dve"].n
    OB = s["pool"].n

    def loads1(q):
        for hc in range(HC):
            if hc >= 2:
                q.wait_ge(s["dve"].h, DB + (hc - 2) + 1)
                q.wait_ge(s["pool"].h, OB + (hc - 2) + 1)
            q.dma_start(out=uf[:, hc % 2],
                        in_=xsrc[hc * P:(hc + 1) * P, :]).then_inc(LD[hc % 2].h, 16)
            LD[hc % 2].inc(16)

    def copyb(v):
        for hc in range(HC):
            v.wait_ge(LD[hc % 2].h, LB[hc % 2] + 16 * (hc // 2 + 1))
            v.tensor_copy(out=S["n_fm"][:, hc], in_=uf[:, hc % 2]).then_inc(s["dve"].h, 1)
            s["dve"].inc()

    def sqb(g):
        for hc in range(HC):
            g.wait_ge(LD[hc % 2].h, LB[hc % 2] + 16 * (hc // 2 + 1))
            if hc >= 2:
                g.wait_ge(s["pe"].h, PB + hc - 1)
            g.tensor_tensor(out=ub[:, 2 + hc % 2], in0=uf[:, hc % 2],
                            in1=uf[:, hc % 2], op=OP.mult).then_inc(s["pool"].h, 1)
            s["pool"].inc()

    def peb(t):
        t.wait_ge(s["dve"].h, 1)
        for hc in range(HC):
            t.wait_ge(s["dve"].h, DB + hc + 1)
            t.wait_ge(s["pool"].h, OB + hc + 1)
            for half in range(2):
                t.matmul(PS[0:1, half * 512:(half + 1) * 512], S["ones_bf"],
                         S["n_fm"][:, hc, half * 512:(half + 1) * 512],
                         start=(hc == 0), stop=(hc == HC - 1))
                ins = t.matmul(PS[0:1, 1024 + half * 512:1024 + (half + 1) * 512],
                               S["ones_bf"], ub[:, 2 + hc % 2, half * 512:(half + 1) * 512],
                               start=(hc == 0), stop=(hc == HC - 1))
            ins.then_inc(s["pe"].h, 1)
            s["pe"].inc()

    def rows1(v):
        v.wait_ge(s["pe"].h, PB + HC)
        v.tensor_scalar_mul(S["m_row"], PS[0:1, 0:1024], 1.0 / H)
        v.tensor_scalar_mul(S["r_row"], PS[0:1, 1024:2048], 1.0 / H)
        v.tensor_tensor(out=S["mr_row"], in0=S["m_row"], in1=S["m_row"], op=OP.mult)
        v.tensor_tensor(out=S["r_row"], in0=S["r_row"], in1=S["mr_row"], op=OP.subtract)
        v.tensor_scalar_add(S["r_row"], S["r_row"], EPS).then_inc(s["dve"].h, 1)
        s["dve"].inc()

    def rows2(a):
        a.wait_ge(s["dve"].h, DB + HC + 1)
        a.activation(out=S["mr_row"], in_=S["r_row"], func=AF.Sqrt).then_inc(s["act"].h, 1)
        s["act"].inc()

    def rows3(v):
        v.wait_ge(s["act"].h, AB + 1)
        v.reciprocal(S["r_row"], S["mr_row"])
        v.tensor_tensor(out=S["mr_row"], in0=S["m_row"], in1=S["r_row"],
                        op=OP.mult).then_inc(s["dve"].h, 1)
        s["dve"].inc()

    def peb2(t):
        t.wait_ge(s["dve"].h, DB + HC + 2)
        for half in range(2):
            t.matmul(PS[:, 2048 + half * 512:2048 + (half + 1) * 512], S["ones_row"],
                     S["r_row"][0:1, half * 512:(half + 1) * 512], start=True, stop=True)
            ins = t.matmul(PS[:, 3072 + half * 512:3072 + (half + 1) * 512], S["ones_row"],
                           S["mr_row"][0:1, half * 512:(half + 1) * 512], start=True, stop=True)
        ins.then_inc(s["pe"].h, 1)
        s["pe"].inc()

    def bev(v):
        v.wait_ge(s["pe"].h, PB + HC + 1)
        v.tensor_copy(out=S["r_bc"], in_=PS[:, 2048:3072])
        v.tensor_copy(out=S["mr_bc"], in_=PS[:, 3072:4096]).then_inc(s["dve"].h, 1)
        s["dve"].inc()

    e.blk.sync(loads1)
    e.blk.vector(copyb)
    e.blk.gpsimd(sqb)
    e.blk.tensor(peb)
    e.blk.vector(rows1)
    e.blk.scalar(rows2)
    e.blk.vector(rows3)
    e.blk.tensor(peb2)
    e.blk.vector(bev)

    DB2 = s["dve"].n
    OB2 = s["pool"].n

    def mk_norm(parity, semname):
        def f(v):
            for hc in range(parity, HC, 2):
                if hc < 2:
                    v.wait_ge(s["dve"].h, DB2)
                if parity == 1 and hc < 2:
                    pass
                if parity == 1:
                    v.wait_ge(s["dve"].h, DB + hc + 1)
                tmp = ub[:, parity]
                v.tensor_tensor(out=tmp, in0=S["n_fm"][:, hc],
                                in1=S["r_bc"], op=OP.mult)
                v.tensor_tensor(out=S["n_fm"][:, hc], in0=tmp,
                                in1=S["mr_bc"], op=OP.subtract).then_inc(s[semname].h, 1)
                s[semname].inc()
        return f

    e.blk.vector(mk_norm(0, "dve"))
    e.blk.gpsimd(mk_norm(1, "pool"))


def _ffnA_prefetch(e, npf, gate=None):
    S, s, d = e.S, e.sem, e.d
    WA = s["WA"]
    e.ffnA_pref = (npf, [x.n for x in WA])

    def pf(g):
        for j in range(npf):
            sl = j % 3
            if j == 0 and gate is not None:
                g.wait_ge(gate[0], gate[1])
            g.dma_start(out=S["w1b"][:, sl], in_=d["w1"][j]).then_inc(WA[sl].h, 16)
            g.dma_start(out=S["w3b"][:, sl], in_=d["w3"][j]).then_inc(WA[sl].h, 16)
            WA[sl].inc(32)

    e.blk.gpsimd(pf)


def _ffnA(e):
    S, s, PS = e.S, e.sem, e.PS
    d = e.d
    WA, HS = s["WA"], s["HS"]
    npf, WAB = getattr(e, "ffnA_pref", (0, None))
    e.ffnA_pref = (0, None)
    if WAB is None:
        WAB = [x.n for x in WA]
    HSB = [x.n for x in HS]
    AB = s["act"].n
    PB = s["pe"].n
    DB = s["dve"].n
    OB = s["pool"].n

    def pe_g(ib):
        return PB + 2 * ib + 1

    def pe_u(ib):
        return PB + 2 * ib + 2

    def act_silu(ib):
        return AB + ib + 1

    def dve_scale(ib):
        return DB + (1 if ib == 0 else 2 * ib)

    def pool_scale(ib):
        return OB + ib + 1

    def dve_mul(k):
        return DB + 2 * (k + 1) + 1 if k < IC - 1 else DB + 2 * IC

    def poolb(g):
        for j in range(IC):
            sl = j % 3
            if j >= npf:
                if j >= 3:
                    g.wait_ge(s["pe"].h, pe_u(j - 3))
                g.dma_start(out=S["w1b"][:, sl], in_=d["w1"][j]).then_inc(WA[sl].h, 16)
                g.dma_start(out=S["w3b"][:, sl], in_=d["w3"][j]).then_inc(WA[sl].h, 16)
                WA[sl].inc(32)
            g.wait_ge(WA[sl].h, WAB[sl] + 32 * (j // 3 + 1))
            sc = S["s3r"][:, j * HC:(j + 1) * HC].unsqueeze(2).broadcast_to([P, HC, P])
            g.tensor_tensor(out=S["w3b"][:, sl], in0=S["w3b"][:, sl], in1=sc,
                            op=OP.mult).then_inc(s["pool"].h, 1)
            s["pool"].inc()

    def dveb(v):
        for ib in range(IC):
            sl = ib % 3
            v.wait_ge(WA[sl].h, WAB[sl] + 32 * (ib // 3 + 1))
            sc = S["s1r"][:, ib * HC:(ib + 1) * HC].unsqueeze(2).broadcast_to([P, HC, P])
            v.tensor_tensor(out=S["w1b"][:, sl], in0=S["w1b"][:, sl], in1=sc,
                            op=OP.mult).then_inc(s["dve"].h, 1)
            s["dve"].inc()
            if ib >= 1:
                k = ib - 1
                v.wait_ge(s["act"].h, act_silu(k))
                v.wait_ge(s["pe"].h, pe_u(k))
                if k >= 2:
                    kk = k - 2
                    v.wait_ge(HS[kk % 2].h, HSB[kk % 2] + 16 * (kk // 2 + 1))
                gb = 2048 * (k % 2)
                v.tensor_tensor(out=S["hb"][:, k % 2], in0=S["sg"][:, k % 2],
                                in1=PS[:, gb + 1024:gb + 2048],
                                op=OP.mult).then_inc(s["dve"].h, 1)
                s["dve"].inc()
        k = IC - 1
        v.wait_ge(s["act"].h, act_silu(k))
        v.wait_ge(s["pe"].h, pe_u(k))
        kk = k - 2
        v.wait_ge(HS[kk % 2].h, HSB[kk % 2] + 16 * (kk // 2 + 1))
        gb = 2048 * (k % 2)
        v.tensor_tensor(out=S["hb"][:, k % 2], in0=S["sg"][:, k % 2],
                        in1=PS[:, gb + 1024:gb + 2048],
                        op=OP.mult).then_inc(s["dve"].h, 1)
        s["dve"].inc()
        assert s["dve"].n == DB + 2 * IC

    def peb(t):
        for ib in range(IC):
            t.wait_ge(s["dve"].h, dve_scale(ib))
            t.wait_ge(s["pool"].h, pool_scale(ib))
            if ib >= 2:
                t.wait_ge(s["act"].h, act_silu(ib - 2))
            gb = 2048 * (ib % 2)
            for hc in range(HC):
                for nh in range(2):
                    ins = t.matmul(PS[:, gb + nh * 512:gb + (nh + 1) * 512],
                                   S["w1b"][:, ib % 3, hc],
                                   S["n_fm"][:, hc, nh * 512:(nh + 1) * 512],
                                   start=(hc == 0), stop=(hc == HC - 1))
            ins.then_inc(s["pe"].h, 1)
            s["pe"].inc()
            for hc in range(HC):
                for nh in range(2):
                    ins = t.matmul(PS[:, gb + 1024 + nh * 512:gb + 1024 + (nh + 1) * 512],
                                   S["w3b"][:, ib % 3, hc],
                                   S["n_fm"][:, hc, nh * 512:(nh + 1) * 512],
                                   start=(hc == 0), stop=(hc == HC - 1))
            ins.then_inc(s["pe"].h, 1)
            s["pe"].inc()

    def actb(a):
        for ib in range(IC):
            a.wait_ge(s["pe"].h, pe_g(ib))
            gb = 2048 * (ib % 2)
            a.activation(out=S["sg"][:, ib % 2], in_=PS[:, gb:gb + 1024],
                         func=AF.Silu).then_inc(s["act"].h, 1)
            s["act"].inc()
            if ib >= 1:
                k = ib - 1
                a.wait_ge(s["dve"].h, dve_mul(k))
                a.dma_start(out=d["hst"][k * P:(k + 1) * P, :],
                            in_=S["hb"][:, k % 2]).then_inc(HS[k % 2].h, 16)
                HS[k % 2].inc(16)
        k = IC - 1
        a.wait_ge(s["dve"].h, dve_mul(k))
        a.dma_start(out=d["hst"][k * P:(k + 1) * P, :],
                    in_=S["hb"][:, k % 2]).then_inc(HS[k % 2].h, 16)
        HS[k % 2].inc(16)

    e.blk.gpsimd(poolb)
    e.blk.vector(dveb)
    e.blk.tensor(peb)
    e.blk.scalar(actb)


def _ffnB(e, pass1, xsrc, xdst):
    S, s, PS = e.S, e.sem, e.PS
    d = e.d
    NCH = IC // 4
    W2, W2S, C2, XS = s["W2"], s["W2S"], s["C2"], s["XS"]
    if not hasattr(e, "w2_last"):
        e.w2_last = {0: 0, 1: 0}
        e.c2_last = {0: 0, 1: 0}
    prev_evac_end = (s["dve"].n, s["pool"].n)
    prev_xs = None

    for tbh in range(2):
        pe_guard = s["pe"].n

        def hload(q, tbh=tbh, pe_guard=pe_guard):
            if tbh == 1:
                q.wait_ge(s["pe"].h, pe_guard)
            for gg in range(8):
                hsrc = d["hst"].rearrange("(ic p) t -> p ic t", p=P)[
                    :, gg * 14:(gg + 1) * 14, tbh * 512:(tbh + 1) * 512]
                q.dma_start(out=S["h_res"][:, gg * 14:(gg + 1) * 14, :],
                            in_=hsrc).then_inc(s["HL"].h, 16)
                s["HL"].inc(16)

        e.blk.sync(hload)
        hl_total = s["HL"].n

        for hcg in range(4):
            dq = pass1 and tbh == 0
            DB = s["dve"].n
            PB = s["pe"].n
            w2_lm = {}

            if dq:
                OB = s["pool"].n

                def scale_op(v, icg, semname, hcg=hcg, w2_lm=w2_lm):
                    sl = icg % 2
                    v.wait_ge(W2S[sl].h, w2_lm[icg])
                    sc = S["s2r"].rearrange("p (i h) -> p i h", h=HC)[
                        :, icg * 4:(icg + 1) * 4, hcg * 8:hcg * 8 + 8]
                    sc = sc.unsqueeze(3).broadcast_to([P, 4, 8, P])
                    v.tensor_tensor(out=S["w2b"][:, sl],
                                    in0=S["w2b"][:, sl], in1=sc,
                                    op=OP.mult).then_inc(s[semname].h, 1)
                    s[semname].inc()

                def w2load_sw(g, hcg=hcg, PB=PB, w2_lm=w2_lm):
                    for icg in range(NCH):
                        sl = icg % 2
                        if e.w2_last[sl]:
                            g.wait_ge(s["pe"].h, e.w2_last[sl])
                        if e.c2_last[sl]:
                            g.wait_ge(C2[sl].h, e.c2_last[sl])
                        g.dma_start(out=S["w2b"][:, sl],
                                    in_=d["w2"][hcg, icg]).then_inc(W2S[sl].h, 16)
                        w2_lm[icg] = W2S[sl].inc(16)
                        e.w2_last[sl] = PB + 4 * (icg + 1)
                        if sl == 1:
                            scale_op(g, icg, "pool")

                def scale_ev(v, w2_lm=w2_lm):
                    for icg in range(0, NCH, 2):
                        scale_op(v, icg, "dve")

                def scale_lm(icg):
                    if icg % 2 == 0:
                        return (s["dve"].h, DB + icg // 2 + 1)
                    return (s["pool"].h, OB + icg // 2 + 1)

                def cacheb(a, hcg=hcg):
                    for icg in range(NCH):
                        sl = icg % 2
                        sh, sv = scale_lm(icg)
                        a.wait_ge(sh, sv)
                        a.dma_start(out=d["w2dq"][hcg, icg],
                                    in_=S["w2b"][:, sl].rearrange("p a b c -> p (a b c)")
                                    ).then_inc(C2[sl].h, 16)
                        e.c2_last[sl] = C2[sl].inc(16)

                e.blk.gpsimd(w2load_sw)
                e.blk.vector(scale_ev)
                e.blk.scalar(cacheb)
            else:
                def w2load_hw(q, hcg=hcg, PB=PB, w2_lm=w2_lm):
                    for icg in range(NCH):
                        sl = icg % 2
                        if e.w2_last[sl]:
                            q.wait_ge(s["pe"].h, e.w2_last[sl])
                        q.dma_start(out=S["w2b"][:, sl].rearrange("p a b c -> p (a b c)"),
                                    in_=d["w2dq"][hcg, icg]).then_inc(W2[sl].h, 16)
                        w2_lm[icg] = W2[sl].inc(16)
                        e.w2_last[sl] = PB + 4 * (icg + 1)

            PFREE = prev_evac_end
            HLT = hl_total

            sclm = scale_lm if dq else None

            def peb(t, dq=dq, w2_lm=w2_lm, PFREE=PFREE, HLT=HLT, sclm=sclm):
                t.wait_ge(s["dve"].h, PFREE[0])
                t.wait_ge(s["pool"].h, PFREE[1])
                t.wait_ge(s["HL"].h, HLT)
                for ic in range(IC):
                    icg, j = divmod(ic, 4)
                    if j == 0:
                        if dq:
                            sh, sv = sclm(icg)
                            t.wait_ge(sh, sv)
                        else:
                            t.wait_ge(W2[icg % 2].h, w2_lm[icg])
                    for hc8 in range(8):
                        ins = t.matmul(PS[:, hc8 * 512:(hc8 + 1) * 512],
                                       S["w2b"][:, icg % 2, j, hc8],
                                       S["h_res"][:, ic, :],
                                       start=(ic == 0), stop=(ic == IC - 1))
                    ins.then_inc(s["pe"].h, 1)
                    s["pe"].inc()

            u8 = S["uf"].rearrange("p a (b t) -> p (a b) t", b=2)
            XLB = s["XL"].n
            pxs = prev_xs

            def xload(q, tbh=tbh, hcg=hcg, pxs=pxs):
                if pxs is not None:
                    q.wait_ge(XS[0].h, pxs[0])
                    q.wait_ge(XS[1].h, pxs[1])
                for hc8 in range(8):
                    hc = hcg * 8 + hc8
                    q.dma_start(out=u8[:, hc8],
                                in_=xsrc[hc * P:(hc + 1) * P, tbh * 512:(tbh + 1) * 512]
                                ).then_inc(s["XL"].h, 16)
                    s["XL"].inc(16)

            e.blk.sync(xload)
            if not dq:
                e.blk.sync(w2load_hw)
            e.blk.tensor(peb)
            PEND = s["pe"].n

            def evac(v, XLB=XLB, PEND=PEND):
                v.wait_ge(s["pe"].h, PEND)
                v.wait_ge(s["XL"].h, XLB + 128)
                for hc8 in range(8):
                    v.scalar_tensor_tensor(out=u8[:, hc8],
                                           in0=PS[:, hc8 * 512:(hc8 + 1) * 512],
                                           scalar=0.5, in1=u8[:, hc8],
                                           op0=OP.mult, op1=OP.add).then_inc(s["dve"].h, 1)
                    s["dve"].inc()

            DEV0 = s["dve"].n - 8 + 8
            DEND = s["dve"].n
            e.blk.vector(evac)
            DEND = s["dve"].n
            prev_evac_end = (s["dve"].n, s["pool"].n)

            def xstore(q, tbh=tbh, hcg=hcg, DEND=DEND):
                for hc8 in range(8):
                    hc = hcg * 8 + hc8
                    q.wait_ge(s["dve"].h, DEND - 8 + hc8 + 1)
                    q.dma_start(out=xdst[hc * P:(hc + 1) * P, tbh * 512:(tbh + 1) * 512],
                                in_=u8[:, hc8]).then_inc(XS[hc8 % 2].h, 16)
                    XS[hc8 % 2].inc(16)

            e.blk.sync(xstore)
            prev_xs = (XS[0].n, XS[1].n)
            if not hasattr(e, "xs_round"):
                e.xs_round = {}
            e.xs_round[(tbh, hcg)] = prev_xs


def _qkv(e):
    S, s, PS = e.S, e.sem, e.PS
    d = e.d
    QW, KS = s["QW"], s["KS"]
    QWB = [x.n for x in QW]
    KSB = [x.n for x in KS]
    WVB = s["WV"].n
    PB = s["pe"].n
    DB = s["dve"].n

    def poolb(g):
        for i in range(8):
            if i >= 2:
                g.wait_ge(s["pe"].h, PB + i - 1)
            g.dma_start(out=S["wqb"][:, i % 2], in_=d["wqk"][8 + i]).then_inc(QW[i % 2].h, 16)
            QW[i % 2].inc(16)
        g.dma_start(out=S["wvb"], in_=d["wv"][0]).then_inc(s["WV"].h, 16)
        s["WV"].inc(16)
        g.wait_ge(KS[0].h, KSB[0] + 16 * 4)
        g.wait_ge(KS[1].h, KSB[1] + 16 * 4)
        g.collective_compute("AllGather", OP.bypass,
                             replica_groups=[[0, 1, 2, 3], [4, 5, 6, 7]],
                             ins=[d["kin"].opt()], outs=[d["kga"].opt()]).then_inc(s["cck"].h)
        s["cck"].inc()
        for i in range(2):
            g.wait_ge(s["pe"].h, PB + 7 + i)
            g.dma_start(out=S["wqb"][:, i % 2], in_=d["wqk"][i]).then_inc(QW[i % 2].h, 16)
            QW[i % 2].inc(16)
        g.wait_ge(s["pe"].h, PB + 16)
        g.dma_start(out=S["wvb"], in_=d["wv"][1]).then_inc(s["WV"].h, 16)
        s["WV"].inc(16)
        for i in range(2, 8):
            g.wait_ge(s["pe"].h, PB + 23 + i)
            g.dma_start(out=S["wqb"][:, i % 2], in_=d["wqk"][i]).then_inc(QW[i % 2].h, 16)
            QW[i % 2].inc(16)
        g.wait_ge(KS[0].h, KSB[0] + 16 * 12)
        g.wait_ge(KS[1].h, KSB[1] + 16 * 12)
        g.collective_compute("AllGather", OP.bypass,
                             replica_groups=[[0, 1, 2, 3], [4, 5, 6, 7]],
                             ins=[d["vin"].opt()], outs=[d["vga"].opt()]).then_inc(s["ccv"].h)
        s["ccv"].inc()

    def peb(t):
        for i in range(8):
            t.wait_ge(QW[i % 2].h, QWB[i % 2] + 16 * (i // 2 + 1))
            if i >= 2:
                t.wait_ge(s["dve"].h, DB + i - 1)
            base = 1024 * (i % 2)
            for nh in range(2):
                for hc in range(HC):
                    ins = t.matmul(PS[:, base + nh * 512:base + (nh + 1) * 512],
                                   S["wqb"][:, i % 2, hc],
                                   S["n_fm"][:, hc, nh * 512:(nh + 1) * 512],
                                   start=(hc == 0), stop=(hc == HC - 1))
            ins.then_inc(s["pe"].h, 1)
            s["pe"].inc()
        for vg in range(2):
            t.wait_ge(s["WV"].h, WVB + 16 * (vg + 1))
            for tc in range(8):
                k = vg * 8 + tc
                if k >= 4:
                    t.wait_ge(s["dve"].h, DB + 8 + (k - 4) + 1)
                for hc in range(HC):
                    ins = t.matmul(PS[:, 2048 + (tc % 4) * 512:2048 + (tc % 4 + 1) * 512],
                                   S["n_fm"][:, hc, tc * P:(tc + 1) * P],
                                   S["wvb"][:, hc],
                                   start=(hc == 0), stop=(hc == HC - 1))
                ins.then_inc(s["pe"].h, 1)
                s["pe"].inc()
        for i in range(8):
            t.wait_ge(QW[i % 2].h, QWB[i % 2] + 16 * (4 + i // 2 + 1))
            t.wait_ge(s["dve"].h, DB + 7 + i)
            base = 1024 * (i % 2)
            for nh in range(2):
                for hc in range(HC):
                    ins = t.matmul(PS[:, base + nh * 512:base + (nh + 1) * 512],
                                   S["wqb"][:, i % 2, hc],
                                   S["n_fm"][:, hc, nh * 512:(nh + 1) * 512],
                                   start=(hc == 0), stop=(hc == HC - 1))
            ins.then_inc(s["pe"].h, 1)
            s["pe"].inc()

    def dveb(v):
        for i in range(8):
            v.wait_ge(s["pe"].h, PB + i + 1)
            if i >= 2:
                v.wait_ge(KS[i % 2].h, KSB[i % 2] + 16 * ((i - 2) // 2 + 1))
            base = 1024 * (i % 2)
            v.tensor_scalar(out=S["kv_st"][:, i % 2], in0=PS[:, base:base + 1024],
                            scalar1=S["bk"][:, i:i + 1], scalar2=None,
                            op0=OP.add).then_inc(s["dve"].h, 1)
            s["dve"].inc()
        for k in range(16):
            vg, tc = divmod(k, 8)
            v.wait_ge(s["pe"].h, PB + 8 + k + 1)
            sl = k % 2
            if k < 2:
                v.wait_ge(KS[sl].h, KSB[sl] + 16 * 4)
            else:
                v.wait_ge(KS[sl].h, KSB[sl] + 16 * (4 + (k - 2) // 2 + 1))
            v.tensor_tensor(out=S["kv_st"][:, sl, 0:512],
                            in0=PS[:, 2048 + (tc % 4) * 512:2048 + (tc % 4 + 1) * 512],
                            in1=S["bvr"][:, vg * 512:(vg + 1) * 512],
                            op=OP.add).then_inc(s["dve"].h, 1)
            s["dve"].inc()
        for i in range(8):
            v.wait_ge(s["pe"].h, PB + 24 + i + 1)
            base = 1024 * (i % 2)
            v.tensor_scalar(out=S["q_bf"][:, i], in0=PS[:, base:base + 1024],
                            scalar1=S["bq"][:, i:i + 1], scalar2=1.0 / 32.0,
                            op0=OP.add, op1=OP.mult).then_inc(s["dve"].h, 1)
            s["dve"].inc()

    def storeb(q):
        for i in range(8):
            q.wait_ge(s["dve"].h, DB + i + 1)
            q.dma_start(out=d["kin"][i * P:(i + 1) * P, :],
                        in_=S["kv_st"][:, i % 2]).then_inc(KS[i % 2].h, 16)
            KS[i % 2].inc(16)
        for k in range(16):
            vg, tc = divmod(k, 8)
            q.wait_ge(s["dve"].h, DB + 8 + k + 1)
            q.dma_start(out=d["vin"][tc * P:(tc + 1) * P, vg * 512:(vg + 1) * 512],
                        in_=S["kv_st"][:, k % 2, 0:512]).then_inc(KS[k % 2].h, 16)
            KS[k % 2].inc(16)

    e.blk.gpsimd(poolb)
    e.blk.tensor(peb)
    e.blk.vector(dveb)
    e.blk.sync(storeb)
    e.qkv_pe_vdone = PB + 24


def _attn_core(e, nqt=8):
    S, s, PS, PSb = e.S, e.sem, e.PS, e.PSb
    d = e.d
    VBs = s["VB"]
    KLB = s["KL"].n

    def kload(q):
        q.wait_ge(s["cck"].h, 1)
        src = d["kga"].rearrange("(r dc p) t -> p dc r t", p=P, dc=DC)
        for dc in range(DC):
            q.dma_start(out=S["K_sb"][:, dc].rearrange("p (r t) -> p r t", r=4),
                        in_=src[:, dc]).then_inc(s["KL"].h, 16)
            s["KL"].inc(16)

    e.blk.sync(kload)
    PB = s["pe"].n
    DB = s["dve"].n
    AB = s["act"].n
    VBB = [x.n for x in VBs]

    POS_T = {}
    POS_A = {}
    _pos = 1
    for _k in range(4):
        _pos += 1
        POS_T[_k] = _pos
    for _j in range(16):
        _pos += 1
        POS_A[2 * _j] = _pos
        _pos += 1
        POS_A[2 * _j + 1] = _pos
        for _t in (2 * _j + 4, 2 * _j + 5):
            if _t < 32:
                _pos += 1
                POS_T[_t] = _pos

    def pe_scores(qt):
        return PB + 73 * qt + 1

    def pe_txp(qt, ktc):
        return PB + 73 * qt + POS_T[ktc]

    def pe_av(qt, ktc):
        return PB + 73 * qt + POS_A[ktc]

    def pe_otxp(qt, dc):
        return PB + 73 * qt + 65 + dc + 1

    def dve_evac(qt, j):
        return DB + 28 * qt + 2 + j + 1

    def dve_div(qt):
        return DB + 28 * qt + 20

    def dve_all(qt):
        return DB + 28 * (qt + 1)

    def act_exp(qt):
        return AB + qt + 1

    def vloads(q):
        q.wait_ge(s["ccv"].h, 1)
        for qt in range(nqt):
            for ktc in range(32):
                lin = 32 * qt + ktc
                if lin >= 4:
                    qt2, ktc2 = divmod(lin - 4, 32)
                    q.wait_ge(s["pe"].h, pe_av(qt2, ktc2))
                sl = lin % 4
                q.dma_start(out=S["vb"][:, sl],
                            in_=d["vga"][ktc * P:(ktc + 1) * P, :]).then_inc(VBs[sl].h, 16)
                VBs[sl].inc(16)

    def peb(t):
        for qt in range(nqt):
            t.wait_ge(s["KL"].h, KLB + 128)
            if qt > 0:
                t.wait_ge(s["act"].h, act_exp(qt - 1))
                t.wait_ge(s["dve"].h, dve_all(qt - 1))
            for kb in range(8):
                for dc in range(DC):
                    ins = t.matmul(PS[:, kb * 512:(kb + 1) * 512],
                                   S["q_bf"][:, dc, qt * P:(qt + 1) * P],
                                   S["K_sb"][:, dc, kb * 512:(kb + 1) * 512],
                                   start=(dc == 0), stop=(dc == DC - 1))
            ins.then_inc(s["pe"].h, 1)
            s["pe"].inc()
            t.wait_ge(s["act"].h, act_exp(qt))

            def txp(ktc):
                j = ktc // 2
                bank = (2 if j % 2 == 0 else 6) + (ktc % 2)
                t.transpose(PSb[:, bank * 1024:bank * 1024 + P],
                            S["p_bf"][:, ktc * P:(ktc + 1) * P],
                            S["ident"]).then_inc(s["pe"].h, 1)
                s["pe"].inc()

            for k0 in range(4):
                txp(k0)
            for j in range(16):
                t.wait_ge(s["dve"].h, dve_evac(qt, j))
                for ktc in (2 * j, 2 * j + 1):
                    lin = 32 * qt + ktc
                    t.wait_ge(VBs[lin % 4].h, VBB[lin % 4] + 16 * (lin // 4 + 1))
                    for dh in range(2):
                        ins = t.matmul(PS[:, dh * 512:(dh + 1) * 512],
                                       S["pT"][:, 2 * (j % 2) + ktc % 2],
                                       S["vb"][:, lin % 4, dh * 512:(dh + 1) * 512],
                                       start=(ktc == 0), stop=(ktc == 31))
                    ins.then_inc(s["pe"].h, 1)
                    s["pe"].inc()
                for t2 in (2 * j + 4, 2 * j + 5):
                    if t2 < 32:
                        txp(t2)
            t.wait_ge(s["dve"].h, dve_div(qt))
            for dc in range(DC):
                if dc >= 2:
                    t.wait_ge(s["dve"].h, DB + 28 * qt + 20 + (dc - 2) + 1)
                t.transpose(PSb[:, (4 + dc % 2) * 1024:(4 + dc % 2) * 1024 + P],
                            S["o_tm"][:, dc * P:(dc + 1) * P],
                            S["ident"]).then_inc(s["pe"].h, 1)
                s["pe"].inc()

    def dveb(v):
        for qt in range(nqt):
            v.wait_ge(s["pe"].h, pe_scores(qt))
            v.reduce_max(S["rl_sb"], PS[:, 0:4096], axis=AX.X,
                         negate=True).then_inc(s["dve"].h, 1)
            s["dve"].inc()
            v.sem_inc(s["dve"].h, 1)
            s["dve"].inc()
            for j in range(16):
                v.wait_ge(s["pe"].h, pe_txp(qt, 2 * j + 1))
                b0 = 2 if j % 2 == 0 else 6
                psrc = PSb[:, b0 * 1024:(b0 + 2) * 1024].rearrange(
                    "p (b c) -> p b c", b=2)[:, :, 0:P]
                v.tensor_copy(out=S["pT"][:, 2 * (j % 2):2 * (j % 2) + 2, :],
                              in_=psrc).then_inc(s["dve"].h, 1)
                s["dve"].inc()
            v.wait_ge(s["act"].h, act_exp(qt))
            v.reciprocal(S["rl_sb"], S["l_sb"]).then_inc(s["dve"].h, 1)
            s["dve"].inc()
            v.wait_ge(s["pe"].h, pe_av(qt, 31))
            v.tensor_scalar(out=S["o_tm"], in0=PS[:, 0:1024], scalar1=S["rl_sb"],
                            scalar2=None, op0=OP.mult).then_inc(s["dve"].h, 1)
            s["dve"].inc()
            for dc in range(DC):
                v.wait_ge(s["pe"].h, pe_otxp(qt, dc))
                v.tensor_copy(out=S["o_fm"][:, dc, qt * P:(qt + 1) * P],
                              in_=PSb[:, (4 + dc % 2) * 1024:(4 + dc % 2) * 1024 + P]
                              ).then_inc(s["dve"].h, 1)
                s["dve"].inc()

    def actb(a):
        for qt in range(nqt):
            a.wait_ge(s["dve"].h, DB + 28 * qt + 2)
            a.activation(out=S["p_bf"], in_=PS[:, 0:4096], func=AF.Exp,
                         bias=S["rl_sb"], scale=1.0,
                         accum_out=S["l_sb"]).then_inc(s["act"].h, 1)
            s["act"].inc()

    e.blk.sync(vloads)
    e.blk.tensor(peb)
    e.blk.vector(dveb)
    e.blk.scalar(actb)
    e.attn_dve_end = DB + 28 * nqt


def _outproj(e, xsrc, xdst):
    S, s, PS = e.S, e.sem, e.PS
    d = e.d
    WO, LD, XS = s["WO"], s["LD"], s["XS"]
    WOB = [x.n for x in WO]
    LB = [x.n for x in LD]
    PB = s["pe"].n
    DB = s["dve"].n

    def poolb(g):
        for hc in range(HC):
            if hc == 0:
                g.wait_ge(s["pe"].h, e.qkv_pe_vdone)
            if hc >= 4:
                g.wait_ge(s["pe"].h, PB + 2 * (hc - 4) + 2)
            g.dma_start(out=S["wob"][:, hc % 4], in_=d["wo"][hc]).then_inc(WO[hc % 4].h, 16)
            WO[hc % 4].inc(16)

    xs_lm = {}

    def syncb(q):
        def store(k):
            q.wait_ge(s["dve"].h, DB + 2 * k + 2)
            sl2 = k % 2
            q.dma_start(out=xdst[k * P:(k + 1) * P, :],
                        in_=S["uf"][:, k % 4]).then_inc(XS[sl2].h, 16)
            xs_lm[k] = XS[sl2].inc(16)

        for hc in range(HC):
            sl = hc % 4
            if hc >= 4:
                q.wait_ge(XS[(hc - 4) % 2].h, xs_lm[hc - 4])
            q.dma_start(out=S["uf"][:, sl],
                        in_=xsrc[hc * P:(hc + 1) * P, :]).then_inc(LD[sl].h, 16)
            LD[sl].inc(16)
            if hc >= 2:
                store(hc - 2)
        store(HC - 2)
        store(HC - 1)

    def peb(t):
        t.wait_ge(s["dve"].h, e.attn_dve_end)
        for hc in range(HC):
            t.wait_ge(WO[hc % 4].h, WOB[hc % 4] + 16 * (hc // 4 + 1))
            if hc >= 2:
                t.wait_ge(s["dve"].h, DB + 2 * (hc - 2) + 2)
            for qh in range(2):
                bank = (hc % 2) * 2 + qh
                for dc in range(DC):
                    ins = t.matmul(PS[:, bank * 512:(bank + 1) * 512],
                                   S["wob"][:, hc % 4, dc],
                                   S["o_fm"][:, dc, qh * 512:(qh + 1) * 512],
                                   start=(dc == 0), stop=(dc == DC - 1))
                ins.then_inc(s["pe"].h, 1)
                s["pe"].inc()

    def dveb(v):
        for hc in range(HC):
            sl = hc % 4
            v.wait_ge(LD[sl].h, LB[sl] + 16 * (hc // 4 + 1))
            for qh in range(2):
                v.wait_ge(s["pe"].h, PB + 2 * hc + qh + 1)
                bank = (hc % 2) * 2 + qh
                v.scalar_tensor_tensor(out=S["uf"][:, sl, qh * 512:(qh + 1) * 512],
                                       in0=PS[:, bank * 512:(bank + 1) * 512],
                                       scalar=S["bo"][:, hc:hc + 1],
                                       in1=S["uf"][:, sl, qh * 512:(qh + 1) * 512],
                                       op0=OP.add, op1=OP.add).then_inc(s["dve"].h, 1)
                s["dve"].inc()

    e.blk.gpsimd(poolb)
    e.blk.sync(syncb)
    e.blk.tensor(peb)
    e.blk.vector(dveb)


_CACHE = {}


def _prepare(inputs):
    w1 = np.asarray(inputs["L0_ffn_w1"], np.float32)
    w3 = np.asarray(inputs["L0_ffn_w3"], np.float32)
    w2 = np.asarray(inputs["L0_ffn_w2"], np.float32)
    s1 = np.asarray(inputs["L0_ffn_w1_scale"], np.float32)
    s3 = np.asarray(inputs["L0_ffn_w3_scale"], np.float32)
    s2 = np.asarray(inputs["L0_ffn_w2_scale"], np.float32)
    wqkv = np.asarray(inputs["L0_attn_Wqkv"], np.float32)
    bqkv = np.asarray(inputs["L0_attn_bqkv"], np.float32)
    wo = np.asarray(inputs["L0_attn_Wo"], np.float32)
    bo = np.asarray(inputs["L0_attn_bo"], np.float32)
    x0 = np.asarray(inputs["L0_attn_x"], np.float32)

    C = np.ascontiguousarray
    shared = {
        "w1t": C(w1.reshape(IC, P, HC, P).transpose(0, 3, 2, 1)),
        "w3t": C(w3.reshape(IC, P, HC, P).transpose(0, 3, 2, 1)),
        "w2t": C(w2.reshape(4, 8, P, IC // 4, 4, P).transpose(0, 3, 5, 4, 1, 2)).reshape(4, IC // 4, P, 4, 1024),
        "s1r": C(np.broadcast_to(s1.reshape(1, IC * HC), (P, IC * HC))),
        "s3r": C(np.broadcast_to(s3.reshape(1, IC * HC), (P, IC * HC))),
        "s2r": C(np.broadcast_to(s2.T.reshape(1, IC * HC), (P, IC * HC))),
        "wqkv_qk": C(wqkv[:, :2048].reshape(HC, P, 16, P).transpose(2, 1, 0, 3)),
        "wqkv_v": C(wqkv[:, 2048:].reshape(HC, P, 2, 512).transpose(2, 1, 0, 3)),
        "bq": C(bqkv[0:1024].reshape(DC, P).T),
        "bk": C(bqkv[1024:2048].reshape(DC, P).T),
        "bvr": C(np.broadcast_to(bqkv[2048:3072][None, :], (P, D))),
        "wo": C(wo.reshape(DC, P, HC, P).transpose(2, 1, 0, 3)),
        "bo": C(bo.reshape(HC, P).T),
        "ident": np.eye(P, dtype=np.float32),
    }
    in_maps = []
    for c in range(NCORES):
        b, t0 = c // 4, (c % 4) * T
        m = dict(shared)
        m["x"] = C(x0[b, t0:t0 + T, :].T)
        in_maps.append(m)
    return in_maps


def kernel(**inputs):
    in_maps = _prepare(inputs)
    if "nc" not in _CACHE:
        _CACHE["nc"] = build()
    nc = _CACHE["nc"]
    res = run_bass_kernel_spmd(nc, in_maps, core_ids=list(range(NCORES)))
    out = np.empty((2, 4096, 4096), np.float32)
    for c in range(NCORES):
        b, t0 = c // 4, (c % 4) * T
        out[b, t0:t0 + T, :] = res.results[c]["out"].T
    return out


# version 18 (final)
# speedup vs baseline: 1.0191x; 1.0191x over previous
import contextlib
import numpy as np
import concourse.bass as bass
import concourse.mybir as mybir
import concourse.bass_utils as _bu
from concourse.bass_utils import run_bass_kernel_spmd

F32 = mybir.dt.float32
BF16 = mybir.dt.bfloat16
AF = mybir.ActivationFunctionType
OP = mybir.AluOpType
AX = mybir.AxisListType

P = 128
T = 1024
H = 4096
HC = 32
D = 1024
DC = 8
I = 14336
IC = 112
EPS = 1e-5
NCORES = 8
SB_BASE = 18432
SBUF_CAP = 229344


class Sem:
    def __init__(self, hdl):
        self.h = hdl
        self.n = 0

    def inc(self, k=1):
        self.n += k
        return self.n


class Env:
    pass


def build(dump=None, detect_races=False):
    nc = bass.Bass("TRN2", target_bir_lowering=False,
                   detect_race_conditions=detect_races)

    d = {}
    d["x"] = nc.dram_tensor("x", [H, T], F32, kind="ExternalInput").ap()
    d["w1"] = nc.dram_tensor("w1t", [IC, P, HC, P], F32, kind="ExternalInput").ap()
    d["w3"] = nc.dram_tensor("w3t", [IC, P, HC, P], F32, kind="ExternalInput").ap()
    d["w2"] = nc.dram_tensor("w2t", [4, IC // 4, P, 4, 1024], F32, kind="ExternalInput").ap()
    d["s1"] = nc.dram_tensor("s1r", [P, IC * HC], F32, kind="ExternalInput").ap()
    d["s3"] = nc.dram_tensor("s3r", [P, IC * HC], F32, kind="ExternalInput").ap()
    d["s2"] = nc.dram_tensor("s2r", [P, IC * HC], F32, kind="ExternalInput").ap()
    d["wqk"] = nc.dram_tensor("wqkv_qk", [16, P, HC, P], F32, kind="ExternalInput").ap()
    d["wv"] = nc.dram_tensor("wqkv_v", [2, P, HC, 512], F32, kind="ExternalInput").ap()
    d["bq"] = nc.dram_tensor("bq", [P, DC], F32, kind="ExternalInput").ap()
    d["bk"] = nc.dram_tensor("bk", [P, DC], F32, kind="ExternalInput").ap()
    d["bv"] = nc.dram_tensor("bvr", [P, D], F32, kind="ExternalInput").ap()
    d["wo"] = nc.dram_tensor("wo", [HC, P, DC, P], F32, kind="ExternalInput").ap()
    d["bo"] = nc.dram_tensor("bo", [P, HC], F32, kind="ExternalInput").ap()
    d["ident"] = nc.dram_tensor("ident", [P, P], F32, kind="ExternalInput").ap()
    d["out"] = nc.dram_tensor("out", [H, T], F32, kind="ExternalOutput").ap()
    dbg_shape = [I, T] if dump == "h" else [H, T]
    d["dbg"] = (nc.dram_tensor("dbg", dbg_shape, F32, kind="ExternalOutput").ap()
                if dump is not None else None)
    if dump == "attn":
        d["dbg_k"] = nc.dram_tensor("dbg_k", [4 * D, T], F32, kind="ExternalOutput").ap()
        d["dbg_v"] = nc.dram_tensor("dbg_v", [4 * T, D], F32, kind="ExternalOutput").ap()
        d["dbg_q"] = nc.dram_tensor("dbg_q", [P, DC * T], F32, kind="ExternalOutput").ap()
        d["dbg_o"] = nc.dram_tensor("dbg_o", [P, DC * T], F32, kind="ExternalOutput").ap()
    if dump == "attn2":
        d["dbg_p"] = nc.dram_tensor("dbg_p", [P, 4 * T], F32, kind="ExternalOutput").ap()
        d["dbg_l"] = nc.dram_tensor("dbg_l", [P, 4], F32, kind="ExternalOutput").ap()
        d["dbg_ot"] = nc.dram_tensor("dbg_ot", [P, D], F32, kind="ExternalOutput").ap()
        d["dbg_o"] = nc.dram_tensor("dbg_o", [P, DC * T], F32, kind="ExternalOutput").ap()
        d["dbg_s"] = nc.dram_tensor("dbg_s", [P, 4 * T], F32, kind="ExternalOutput").ap()

    d["xr0"] = nc.dram_tensor("xr0", [H, T], F32, kind="Internal").ap()
    d["xr1"] = nc.dram_tensor("xr1", [H, T], F32, kind="Internal").ap()
    d["hst"] = nc.dram_tensor("hst", [I, T], BF16, kind="Internal").ap()
    d["w2dq"] = nc.dram_tensor("w2dq", [4, IC // 4, P, 4096], BF16, kind="Internal").ap()
    d["kin"] = nc.dram_tensor("kin", [D, T], BF16, kind="Internal").ap()
    d["kga"] = nc.dram_tensor("kga", [4 * D, T], BF16, kind="Internal").ap()
    d["vin"] = nc.dram_tensor("vin", [T, D], BF16, kind="Internal").ap()
    d["vga"] = nc.dram_tensor("vga", [4 * T, D], BF16, kind="Internal").ap()

    off = {}
    cur = SB_BASE

    def place(name, nbytes, align=32):
        nonlocal cur
        cur = (cur + align - 1) // align * align
        off[name] = cur
        cur += nbytes

    place("s1r", HC * IC * 2)
    place("s3r", HC * IC * 2)
    place("s2r", HC * IC * 2)
    place("bq", DC * 4)
    place("bk", DC * 4)
    place("bo", HC * 4)
    place("bvr", D * 4)
    place("ident", P * 2)
    place("ones_bf", 2)
    place("ones_row", P * 4)
    place("m_row", T * 4)
    place("r_row", T * 4)
    place("mr_row", T * 4)
    place("r_bc", T * 4)
    place("mr_bc", T * 4)
    place("l_sb", 4)
    place("rl_sb", 4)
    place("uf", 4 * T * 4)
    place("ub", 4 * T * 2)
    place("BIG", 0)
    BIG = off["BIG"]
    assert BIG + 135168 <= SBUF_CAP, (BIG,)

    lay = [
        ("s1r", [P, IC * HC], BF16, off["s1r"]),
        ("s3r", [P, IC * HC], BF16, off["s3r"]),
        ("s2r", [P, IC * HC], BF16, off["s2r"]),
        ("bq", [P, DC], F32, off["bq"]),
        ("bk", [P, DC], F32, off["bk"]),
        ("bo", [P, HC], F32, off["bo"]),
        ("bvr", [P, D], F32, off["bvr"]),
        ("ident", [P, P], BF16, off["ident"]),
        ("ones_bf", [P, 1], BF16, off["ones_bf"]),
        ("ones_row", [1, P], F32, off["ones_row"]),
        ("m_row", [1, T], F32, off["m_row"]),
        ("r_row", [1, T], F32, off["r_row"]),
        ("mr_row", [1, T], F32, off["mr_row"]),
        ("r_bc", [P, T], F32, off["r_bc"]),
        ("mr_bc", [P, T], F32, off["mr_bc"]),
        ("l_sb", [P, 1], F32, off["l_sb"]),
        ("rl_sb", [P, 1], F32, off["rl_sb"]),
        ("uf", [P, 4, T], F32, off["uf"]),
        ("ub", [P, 4, T], BF16, off["ub"]),
        ("n_fm", [P, HC, T], BF16, BIG + 0),
        ("w1b", [P, 3, HC, P], BF16, BIG + 65536),
        ("w3b", [P, 3, HC, P], BF16, BIG + 90112),
        ("sg", [P, 2, T], BF16, BIG + 114688),
        ("hb", [P, 2, T], BF16, BIG + 118784),
        ("h_res", [P, IC, 512], BF16, BIG + 0),
        ("w2b", [P, 2, 4, 8, P], BF16, BIG + 114688),
        ("q_bf", [P, DC, T], BF16, BIG + 65536),
        ("wqb", [P, 2, HC, P], BF16, BIG + 81920),
        ("wvb", [P, HC, 512], BF16, BIG + 98304),
        ("kv_st", [P, 2, T], BF16, BIG + 131072),
        ("K_sb", [P, DC, 4 * T], BF16, BIG + 0),
        ("o_fm", [P, DC, T], BF16, BIG + 81920),
        ("p_bf", [P, 4 * T], BF16, BIG + 98304),
        ("pT", [P, 4, P], BF16, BIG + 106496),
        ("vb", [P, 4, D], BF16, BIG + 107520),
        ("o_tm", [P, D], BF16, BIG + 115712),
        ("wob", [P, 4, DC, P], BF16, BIG + 117760),
    ]
    S = {}
    for name, shape, dtype, o in lay:
        S[name] = nc.alloc_sbuf_tensor_at("sb_" + name, shape, dtype, offset=o).ap()

    PS = nc.alloc_psum_tensor("PS", [P, 4096], F32).ap()
    PSb = PS.bitcast(BF16)

    ctx = contextlib.ExitStack()

    def mksem(name):
        return Sem(ctx.enter_context(nc.semaphore("sm_" + name)))

    s = {}
    for name in ["dw", "dxm", "pe", "dve", "act", "pool", "gj", "cck", "ccv",
                 "WV", "XL", "KL", "HL", "DBG"]:
        s[name] = mksem(name)
    for name, k in [("LD", 4), ("WA", 3), ("HS", 2), ("W2", 4), ("W2S", 4),
                    ("C2", 4), ("XS", 2), ("QW", 2), ("KS", 2), ("VB", 4),
                    ("WO", 4)]:
        s[name] = [mksem(f"{name}{i}") for i in range(k)]

    dma_sems = ([s["dw"], s["dxm"], s["WV"], s["XL"], s["KL"], s["HL"]]
                + s["LD"] + s["WA"] + s["HS"] + s["W2"] + s["W2S"] + s["C2"]
                + s["XS"] + s["QW"] + s["KS"] + s["VB"] + s["WO"])

    blk = ctx.enter_context(nc.Block())

    e = Env()
    e.nc, e.blk, e.S, e.PS, e.PSb = nc, blk, S, PS, PSb
    e.sem = s
    e.dma_sems = dma_sems
    e.d = d
    e.njoin = 0
    _emit(e, dump)
    ctx.close()
    return nc


def _join(e, skip_pool=False):
    s = e.sem
    nparts = 4 if skip_pool else 5
    tgt = s["gj"].n + nparts

    def syncb(q):
        for ds in e.dma_sems:
            if ds.n:
                q.wait_ge(ds.h, ds.n)
        q.sem_inc(s["gj"].h, 1)
        q.wait_ge(s["gj"].h, tgt)

    def plain(q):
        q.sem_inc(s["gj"].h, 1)
        q.wait_ge(s["gj"].h, tgt)

    e.blk.sync(syncb)
    if not skip_pool:
        e.blk.gpsimd(plain)
    e.blk.scalar(plain)
    e.blk.tensor(plain)
    e.blk.vector(plain)
    s["gj"].n = tgt


def _emit(e, dump):
    d, S, s = e.d, e.S, e.sem

    def initpool(g):
        g.dma_start(out=S["ident"], in_=d["ident"]).then_inc(s["dw"].h, 16)
        g.dma_start(out=S["s1r"], in_=d["s1"]).then_inc(s["dw"].h, 16)
        g.dma_start(out=S["s3r"], in_=d["s3"]).then_inc(s["dw"].h, 16)
        g.dma_start(out=S["s2r"], in_=d["s2"]).then_inc(s["dw"].h, 16)
        s["dw"].inc(64)

    def initsync(q):
        q.dma_start(out=S["bq"], in_=d["bq"]).then_inc(s["dxm"].h, 16)
        q.dma_start(out=S["bk"], in_=d["bk"]).then_inc(s["dxm"].h, 16)
        q.dma_start(out=S["bo"], in_=d["bo"]).then_inc(s["dxm"].h, 16)
        q.dma_start(out=S["bvr"], in_=d["bv"]).then_inc(s["dxm"].h, 16)
        s["dxm"].inc(64)

    def initvec(v):
        v.memset(S["ones_bf"], 1.0)
        v.memset(S["ones_row"], 1.0).then_inc(s["dve"].h, 1)
        s["dve"].inc()

    e.blk.gpsimd(initpool)
    e.blk.sync(initsync)
    e.blk.vector(initvec)
    _ffnA_prefetch(e, 3)

    _ln(e, d["x"])
    _join(e)
    if dump == "ln":
        def dmpn(g):
            g.dma_start(out=d["dbg"].rearrange("(hc p) t -> p hc t", p=P),
                        in_=S["n_fm"]).then_inc(s["DBG"].h, 16)
            g.dma_start(out=d["out"], in_=d["x"]).then_inc(s["DBG"].h, 16)
            s["DBG"].inc(32)
            g.wait_ge(s["DBG"].h, s["DBG"].n)
        e.blk.gpsimd(dmpn)
        return
    _ffnA(e)
    _join(e)
    if dump == "h":
        def dmph(g):
            g.dma_start(out=d["dbg"], in_=d["hst"]).then_inc(s["DBG"].h, 16)
            g.dma_start(out=d["out"], in_=d["x"]).then_inc(s["DBG"].h, 16)
            s["DBG"].inc(32)
            g.wait_ge(s["DBG"].h, s["DBG"].n)
        e.blk.gpsimd(dmph)
        return
    _ffnB(e, pass1=True, xsrc=d["x"], xdst=d["xr0"])

    if dump == "x1":
        _dump(e, d["xr0"])
        return

    _ln(e, d["xr0"])
    _join(e)
    _qkv(e)
    _join(e)
    _attn_core(e, nqt=(2 if dump == "attn2" else 8))
    _join(e)
    if dump == "attn2":
        def dmpa2(g):
            XSs = s["DBG"]
            g.dma_start(out=d["dbg_p"], in_=S["p_bf"]).then_inc(XSs.h, 16)
            with e.nc.allow_non_contiguous_dma(reason="debug dump"):
                g.dma_start(out=d["dbg_l"][:, 0:1], in_=S["l_sb"]).then_inc(XSs.h, 16)
                g.dma_start(out=d["dbg_l"][:, 1:2], in_=S["rl_sb"]).then_inc(XSs.h, 16)
                g.dma_start(out=d["dbg_l"][:, 2:3], in_=S["r_bc"][:, 0:1]).then_inc(XSs.h, 16)
            g.dma_start(out=d["dbg_ot"], in_=S["o_tm"]).then_inc(XSs.h, 16)
            g.dma_start(out=d["dbg_o"],
                        in_=S["o_fm"].rearrange("p a b -> p (a b)")).then_inc(XSs.h, 16)
            g.dma_start(out=d["out"], in_=d["xr0"]).then_inc(XSs.h, 16)
            g.dma_start(out=d["dbg_s"],
                        in_=S["uf"].rearrange("p a b -> p (a b)")).then_inc(XSs.h, 16)
            g.dma_start(out=d["dbg"], in_=d["xr0"]).then_inc(XSs.h, 16)
            XSs.inc(144)
            g.wait_ge(XSs.h, XSs.n)
        e.blk.gpsimd(dmpa2)
        return
    _outproj(e, xsrc=d["xr0"], xdst=d["xr1"])
    _join(e)

    if dump == "x2":
        _dump(e, d["xr1"])
        return
    if dump == "attn":
        def dmpa(g):
            XSs = s["DBG"]
            g.dma_start(out=d["dbg"], in_=d["xr1"]).then_inc(XSs.h, 16)
            g.dma_start(out=d["out"], in_=d["xr1"]).then_inc(XSs.h, 16)
            g.dma_start(out=d["dbg_k"], in_=d["kga"]).then_inc(XSs.h, 16)
            g.dma_start(out=d["dbg_v"], in_=d["vga"]).then_inc(XSs.h, 16)
            g.dma_start(out=d["dbg_q"],
                        in_=S["q_bf"].rearrange("p a b -> p (a b)")).then_inc(XSs.h, 16)
            g.dma_start(out=d["dbg_o"],
                        in_=S["o_fm"].rearrange("p a b -> p (a b)")).then_inc(XSs.h, 16)
            XSs.inc(96)
            g.wait_ge(XSs.h, XSs.n)
        e.blk.gpsimd(dmpa)
        return

    _ln(e, d["xr1"])
    _join(e)
    _ffnA(e)
    _join(e)
    _ffnB(e, pass1=False, xsrc=d["xr1"], xdst=d["out"])
    _join(e)


def _dump(e, src):
    s = e.sem

    def dmp(q):
        q.dma_start(out=e.d["dbg"], in_=src).then_inc(s["DBG"].h, 16)
        q.dma_start(out=e.d["out"], in_=src).then_inc(s["DBG"].h, 16)
        s["DBG"].inc(32)
        q.wait_ge(s["DBG"].h, s["DBG"].n)

    e.blk.gpsimd(dmp)


def _ln(e, xsrc):
    S, s, PS = e.S, e.sem, e.PS
    uf, ub = S["uf"], S["ub"]
    LD = s["LD"]
    LB = [x.n for x in LD]
    AB = s["act"].n
    PB = s["pe"].n
    DB = s["dve"].n
    OB = s["pool"].n

    def loads1(q):
        for hc in range(HC):
            if hc >= 2:
                q.wait_ge(s["dve"].h, DB + (hc - 2) + 1)
                q.wait_ge(s["pool"].h, OB + (hc - 2) + 1)
            q.dma_start(out=uf[:, hc % 2],
                        in_=xsrc[hc * P:(hc + 1) * P, :]).then_inc(LD[hc % 2].h, 16)
            LD[hc % 2].inc(16)

    def copyb(v):
        for hc in range(HC):
            v.wait_ge(LD[hc % 2].h, LB[hc % 2] + 16 * (hc // 2 + 1))
            v.tensor_copy(out=S["n_fm"][:, hc], in_=uf[:, hc % 2]).then_inc(s["dve"].h, 1)
            s["dve"].inc()

    def sqb(g):
        for hc in range(HC):
            g.wait_ge(LD[hc % 2].h, LB[hc % 2] + 16 * (hc // 2 + 1))
            if hc >= 2:
                g.wait_ge(s["pe"].h, PB + hc - 1)
            g.tensor_tensor(out=ub[:, 2 + hc % 2], in0=uf[:, hc % 2],
                            in1=uf[:, hc % 2], op=OP.mult).then_inc(s["pool"].h, 1)
            s["pool"].inc()

    def peb(t):
        t.wait_ge(s["dve"].h, 1)
        for hc in range(HC):
            t.wait_ge(s["dve"].h, DB + hc + 1)
            t.wait_ge(s["pool"].h, OB + hc + 1)
            for half in range(2):
                t.matmul(PS[0:1, half * 512:(half + 1) * 512], S["ones_bf"],
                         S["n_fm"][:, hc, half * 512:(half + 1) * 512],
                         start=(hc == 0), stop=(hc == HC - 1))
                ins = t.matmul(PS[0:1, 1024 + half * 512:1024 + (half + 1) * 512],
                               S["ones_bf"], ub[:, 2 + hc % 2, half * 512:(half + 1) * 512],
                               start=(hc == 0), stop=(hc == HC - 1))
            ins.then_inc(s["pe"].h, 1)
            s["pe"].inc()

    def rows1(v):
        v.wait_ge(s["pe"].h, PB + HC)
        v.tensor_scalar_mul(S["m_row"], PS[0:1, 0:1024], 1.0 / H)
        v.tensor_scalar_mul(S["r_row"], PS[0:1, 1024:2048], 1.0 / H)
        v.tensor_tensor(out=S["mr_row"], in0=S["m_row"], in1=S["m_row"], op=OP.mult)
        v.tensor_tensor(out=S["r_row"], in0=S["r_row"], in1=S["mr_row"], op=OP.subtract)
        v.tensor_scalar_add(S["r_row"], S["r_row"], EPS).then_inc(s["dve"].h, 1)
        s["dve"].inc()

    def rows2(a):
        a.wait_ge(s["dve"].h, DB + HC + 1)
        a.activation(out=S["mr_row"], in_=S["r_row"], func=AF.Sqrt).then_inc(s["act"].h, 1)
        s["act"].inc()

    def rows3(v):
        v.wait_ge(s["act"].h, AB + 1)
        v.reciprocal(S["r_row"], S["mr_row"])
        v.tensor_tensor(out=S["mr_row"], in0=S["m_row"], in1=S["r_row"],
                        op=OP.mult).then_inc(s["dve"].h, 1)
        s["dve"].inc()

    def peb2(t):
        t.wait_ge(s["dve"].h, DB + HC + 2)
        for half in range(2):
            t.matmul(PS[:, 2048 + half * 512:2048 + (half + 1) * 512], S["ones_row"],
                     S["r_row"][0:1, half * 512:(half + 1) * 512], start=True, stop=True)
            ins = t.matmul(PS[:, 3072 + half * 512:3072 + (half + 1) * 512], S["ones_row"],
                           S["mr_row"][0:1, half * 512:(half + 1) * 512], start=True, stop=True)
        ins.then_inc(s["pe"].h, 1)
        s["pe"].inc()

    def bev(v):
        v.wait_ge(s["pe"].h, PB + HC + 1)
        v.tensor_copy(out=S["r_bc"], in_=PS[:, 2048:3072])
        v.tensor_copy(out=S["mr_bc"], in_=PS[:, 3072:4096]).then_inc(s["dve"].h, 1)
        s["dve"].inc()

    e.blk.sync(loads1)
    e.blk.vector(copyb)
    e.blk.gpsimd(sqb)
    e.blk.tensor(peb)
    e.blk.vector(rows1)
    e.blk.scalar(rows2)
    e.blk.vector(rows3)
    e.blk.tensor(peb2)
    e.blk.vector(bev)

    DB2 = s["dve"].n
    OB2 = s["pool"].n

    def mk_norm(parity, semname):
        def f(v):
            for hc in range(parity, HC, 2):
                if hc < 2:
                    v.wait_ge(s["dve"].h, DB2)
                if parity == 1 and hc < 2:
                    pass
                if parity == 1:
                    v.wait_ge(s["dve"].h, DB + hc + 1)
                tmp = ub[:, parity]
                v.tensor_tensor(out=tmp, in0=S["n_fm"][:, hc],
                                in1=S["r_bc"], op=OP.mult)
                v.tensor_tensor(out=S["n_fm"][:, hc], in0=tmp,
                                in1=S["mr_bc"], op=OP.subtract).then_inc(s[semname].h, 1)
                s[semname].inc()
        return f

    e.blk.vector(mk_norm(0, "dve"))
    e.blk.gpsimd(mk_norm(1, "pool"))


def _ffnA_prefetch(e, npf, gate=None):
    S, s, d = e.S, e.sem, e.d
    WA = s["WA"]
    e.ffnA_pref = (npf, [x.n for x in WA])

    def pf(g):
        for j in range(npf):
            sl = j % 3
            if j == 0 and gate is not None:
                g.wait_ge(gate[0], gate[1])
            g.dma_start(out=S["w1b"][:, sl], in_=d["w1"][j]).then_inc(WA[sl].h, 16)
            g.dma_start(out=S["w3b"][:, sl], in_=d["w3"][j]).then_inc(WA[sl].h, 16)
            WA[sl].inc(32)

    e.blk.gpsimd(pf)


def _ffnA(e):
    S, s, PS = e.S, e.sem, e.PS
    d = e.d
    WA, HS = s["WA"], s["HS"]
    npf, WAB = getattr(e, "ffnA_pref", (0, None))
    e.ffnA_pref = (0, None)
    if WAB is None:
        WAB = [x.n for x in WA]
    HSB = [x.n for x in HS]
    AB = s["act"].n
    PB = s["pe"].n
    DB = s["dve"].n
    OB = s["pool"].n

    def pe_g(ib):
        return PB + 2 * ib + 1

    def pe_u(ib):
        return PB + 2 * ib + 2

    def act_silu(ib):
        return AB + ib + 1

    def dve_scale(ib):
        return DB + (1 if ib == 0 else 2 * ib)

    def pool_scale(ib):
        return OB + ib + 1

    def dve_mul(k):
        return DB + 2 * (k + 1) + 1 if k < IC - 1 else DB + 2 * IC

    def poolb(g):
        for j in range(IC):
            sl = j % 3
            if j >= npf:
                if j >= 3:
                    g.wait_ge(s["pe"].h, pe_u(j - 3))
                g.dma_start(out=S["w1b"][:, sl], in_=d["w1"][j]).then_inc(WA[sl].h, 16)
                g.dma_start(out=S["w3b"][:, sl], in_=d["w3"][j]).then_inc(WA[sl].h, 16)
                WA[sl].inc(32)
            g.wait_ge(WA[sl].h, WAB[sl] + 32 * (j // 3 + 1))
            sc = S["s3r"][:, j * HC:(j + 1) * HC].unsqueeze(2).broadcast_to([P, HC, P])
            g.tensor_tensor(out=S["w3b"][:, sl], in0=S["w3b"][:, sl], in1=sc,
                            op=OP.mult).then_inc(s["pool"].h, 1)
            s["pool"].inc()

    def dveb(v):
        for ib in range(IC):
            sl = ib % 3
            v.wait_ge(WA[sl].h, WAB[sl] + 32 * (ib // 3 + 1))
            sc = S["s1r"][:, ib * HC:(ib + 1) * HC].unsqueeze(2).broadcast_to([P, HC, P])
            v.tensor_tensor(out=S["w1b"][:, sl], in0=S["w1b"][:, sl], in1=sc,
                            op=OP.mult).then_inc(s["dve"].h, 1)
            s["dve"].inc()
            if ib >= 1:
                k = ib - 1
                v.wait_ge(s["act"].h, act_silu(k))
                v.wait_ge(s["pe"].h, pe_u(k))
                if k >= 2:
                    kk = k - 2
                    v.wait_ge(HS[kk % 2].h, HSB[kk % 2] + 16 * (kk // 2 + 1))
                gb = 2048 * (k % 2)
                v.tensor_tensor(out=S["hb"][:, k % 2], in0=S["sg"][:, k % 2],
                                in1=PS[:, gb + 1024:gb + 2048],
                                op=OP.mult).then_inc(s["dve"].h, 1)
                s["dve"].inc()
        k = IC - 1
        v.wait_ge(s["act"].h, act_silu(k))
        v.wait_ge(s["pe"].h, pe_u(k))
        kk = k - 2
        v.wait_ge(HS[kk % 2].h, HSB[kk % 2] + 16 * (kk // 2 + 1))
        gb = 2048 * (k % 2)
        v.tensor_tensor(out=S["hb"][:, k % 2], in0=S["sg"][:, k % 2],
                        in1=PS[:, gb + 1024:gb + 2048],
                        op=OP.mult).then_inc(s["dve"].h, 1)
        s["dve"].inc()
        assert s["dve"].n == DB + 2 * IC

    def peb(t):
        for ib in range(IC):
            t.wait_ge(s["dve"].h, dve_scale(ib))
            t.wait_ge(s["pool"].h, pool_scale(ib))
            if ib >= 2:
                t.wait_ge(s["act"].h, act_silu(ib - 2))
            gb = 2048 * (ib % 2)
            for hc in range(HC):
                for nh in range(2):
                    ins = t.matmul(PS[:, gb + nh * 512:gb + (nh + 1) * 512],
                                   S["w1b"][:, ib % 3, hc],
                                   S["n_fm"][:, hc, nh * 512:(nh + 1) * 512],
                                   start=(hc == 0), stop=(hc == HC - 1))
            ins.then_inc(s["pe"].h, 1)
            s["pe"].inc()
            for hc in range(HC):
                for nh in range(2):
                    ins = t.matmul(PS[:, gb + 1024 + nh * 512:gb + 1024 + (nh + 1) * 512],
                                   S["w3b"][:, ib % 3, hc],
                                   S["n_fm"][:, hc, nh * 512:(nh + 1) * 512],
                                   start=(hc == 0), stop=(hc == HC - 1))
            ins.then_inc(s["pe"].h, 1)
            s["pe"].inc()

    def actb(a):
        for ib in range(IC):
            a.wait_ge(s["pe"].h, pe_g(ib))
            gb = 2048 * (ib % 2)
            a.activation(out=S["sg"][:, ib % 2], in_=PS[:, gb:gb + 1024],
                         func=AF.Silu).then_inc(s["act"].h, 1)
            s["act"].inc()
            if ib >= 1:
                k = ib - 1
                a.wait_ge(s["dve"].h, dve_mul(k))
                a.dma_start(out=d["hst"][k * P:(k + 1) * P, :],
                            in_=S["hb"][:, k % 2]).then_inc(HS[k % 2].h, 16)
                HS[k % 2].inc(16)
        k = IC - 1
        a.wait_ge(s["dve"].h, dve_mul(k))
        a.dma_start(out=d["hst"][k * P:(k + 1) * P, :],
                    in_=S["hb"][:, k % 2]).then_inc(HS[k % 2].h, 16)
        HS[k % 2].inc(16)

    e.blk.gpsimd(poolb)
    e.blk.vector(dveb)
    e.blk.tensor(peb)
    e.blk.scalar(actb)


def _ffnB(e, pass1, xsrc, xdst):
    S, s, PS = e.S, e.sem, e.PS
    d = e.d
    NCH = IC // 4
    W2, W2S, C2, XS = s["W2"], s["W2S"], s["C2"], s["XS"]
    if not hasattr(e, "w2_last"):
        e.w2_last = {0: 0, 1: 0}
        e.c2_last = {0: 0, 1: 0}
    prev_evac_end = s["dve"].n
    prev_xs = None

    for tbh in range(2):
        pe_guard = s["pe"].n

        def hload(q, tbh=tbh, pe_guard=pe_guard):
            if tbh == 1:
                q.wait_ge(s["pe"].h, pe_guard)
            for gg in range(8):
                hsrc = d["hst"].rearrange("(ic p) t -> p ic t", p=P)[
                    :, gg * 14:(gg + 1) * 14, tbh * 512:(tbh + 1) * 512]
                q.dma_start(out=S["h_res"][:, gg * 14:(gg + 1) * 14, :],
                            in_=hsrc).then_inc(s["HL"].h, 16)
                s["HL"].inc(16)

        e.blk.sync(hload)
        hl_total = s["HL"].n

        for hcg in range(4):
            dq = pass1 and tbh == 0
            DB = s["dve"].n
            PB = s["pe"].n
            w2_lm = {}

            if dq:
                OB = s["pool"].n

                def scale_op(v, icg, semname, hcg=hcg, w2_lm=w2_lm):
                    sl = icg % 2
                    v.wait_ge(W2S[sl].h, w2_lm[icg])
                    sc = S["s2r"].rearrange("p (i h) -> p i h", h=HC)[
                        :, icg * 4:(icg + 1) * 4, hcg * 8:hcg * 8 + 8]
                    sc = sc.unsqueeze(3).broadcast_to([P, 4, 8, P])
                    v.tensor_tensor(out=S["w2b"][:, sl],
                                    in0=S["w2b"][:, sl], in1=sc,
                                    op=OP.mult).then_inc(s[semname].h, 1)
                    s[semname].inc()

                def w2load_sw(g, hcg=hcg, PB=PB, w2_lm=w2_lm):
                    for icg in range(NCH):
                        sl = icg % 2
                        if e.w2_last[sl]:
                            g.wait_ge(s["pe"].h, e.w2_last[sl])
                        if e.c2_last[sl]:
                            g.wait_ge(C2[sl].h, e.c2_last[sl])
                        g.dma_start(out=S["w2b"][:, sl],
                                    in_=d["w2"][hcg, icg]).then_inc(W2S[sl].h, 16)
                        w2_lm[icg] = W2S[sl].inc(16)
                        e.w2_last[sl] = PB + 4 * (icg + 1)
                        if sl == 1:
                            scale_op(g, icg, "pool")

                def scale_ev(v, w2_lm=w2_lm):
                    for icg in range(0, NCH, 2):
                        scale_op(v, icg, "dve")

                def scale_lm(icg):
                    if icg % 2 == 0:
                        return (s["dve"].h, DB + icg // 2 + 1)
                    return (s["pool"].h, OB + icg // 2 + 1)

                def cacheb(a, hcg=hcg):
                    for icg in range(NCH):
                        sl = icg % 2
                        sh, sv = scale_lm(icg)
                        a.wait_ge(sh, sv)
                        a.dma_start(out=d["w2dq"][hcg, icg],
                                    in_=S["w2b"][:, sl].rearrange("p a b c -> p (a b c)")
                                    ).then_inc(C2[sl].h, 16)
                        e.c2_last[sl] = C2[sl].inc(16)

                e.blk.gpsimd(w2load_sw)
                e.blk.vector(scale_ev)
                e.blk.scalar(cacheb)
            else:
                def w2load_hw(q, hcg=hcg, PB=PB, w2_lm=w2_lm):
                    for icg in range(NCH):
                        sl = icg % 2
                        if e.w2_last[sl]:
                            q.wait_ge(s["pe"].h, e.w2_last[sl])
                        q.dma_start(out=S["w2b"][:, sl].rearrange("p a b c -> p (a b c)"),
                                    in_=d["w2dq"][hcg, icg]).then_inc(W2[sl].h, 16)
                        w2_lm[icg] = W2[sl].inc(16)
                        e.w2_last[sl] = PB + 4 * (icg + 1)

                e.blk.sync(w2load_hw)

            PFREE = prev_evac_end
            HLT = hl_total

            sclm = scale_lm if dq else None

            def peb(t, dq=dq, w2_lm=w2_lm, PFREE=PFREE, HLT=HLT, sclm=sclm):
                t.wait_ge(s["dve"].h, PFREE)
                t.wait_ge(s["HL"].h, HLT)
                for ic in range(IC):
                    icg, j = divmod(ic, 4)
                    if j == 0:
                        if dq:
                            sh, sv = sclm(icg)
                            t.wait_ge(sh, sv)
                        else:
                            t.wait_ge(W2[icg % 2].h, w2_lm[icg])
                    for hc8 in range(8):
                        ins = t.matmul(PS[:, hc8 * 512:(hc8 + 1) * 512],
                                       S["w2b"][:, icg % 2, j, hc8],
                                       S["h_res"][:, ic, :],
                                       start=(ic == 0), stop=(ic == IC - 1))
                    ins.then_inc(s["pe"].h, 1)
                    s["pe"].inc()

            e.blk.tensor(peb)

            u8 = S["uf"].rearrange("p a (b t) -> p (a b) t", b=2)
            XLB = s["XL"].n
            pxs = prev_xs

            def xload(q, tbh=tbh, hcg=hcg, pxs=pxs):
                if pxs is not None:
                    q.wait_ge(XS[0].h, pxs[0])
                    q.wait_ge(XS[1].h, pxs[1])
                for hc8 in range(8):
                    hc = hcg * 8 + hc8
                    q.dma_start(out=u8[:, hc8],
                                in_=xsrc[hc * P:(hc + 1) * P, tbh * 512:(tbh + 1) * 512]
                                ).then_inc(s["XL"].h, 16)
                    s["XL"].inc(16)

            e.blk.sync(xload)
            PEND = s["pe"].n

            def evac(v, XLB=XLB, PEND=PEND):
                v.wait_ge(s["pe"].h, PEND)
                v.wait_ge(s["XL"].h, XLB + 128)
                for hc8 in range(8):
                    v.scalar_tensor_tensor(out=u8[:, hc8],
                                           in0=PS[:, hc8 * 512:(hc8 + 1) * 512],
                                           scalar=0.5, in1=u8[:, hc8],
                                           op0=OP.mult, op1=OP.add).then_inc(s["dve"].h, 1)
                    s["dve"].inc()

            e.blk.vector(evac)
            prev_evac_end = DEND = s["dve"].n

            def xstore(q, tbh=tbh, hcg=hcg, DEND=DEND):
                for hc8 in range(8):
                    hc = hcg * 8 + hc8
                    q.wait_ge(s["dve"].h, DEND - 8 + hc8 + 1)
                    q.dma_start(out=xdst[hc * P:(hc + 1) * P, tbh * 512:(tbh + 1) * 512],
                                in_=u8[:, hc8]).then_inc(XS[hc8 % 2].h, 16)
                    XS[hc8 % 2].inc(16)

            e.blk.sync(xstore)
            prev_xs = (XS[0].n, XS[1].n)
            if not hasattr(e, "xs_round"):
                e.xs_round = {}
            e.xs_round[(tbh, hcg)] = prev_xs


def _qkv(e):
    S, s, PS = e.S, e.sem, e.PS
    d = e.d
    QW, KS = s["QW"], s["KS"]
    QWB = [x.n for x in QW]
    KSB = [x.n for x in KS]
    WVB = s["WV"].n
    PB = s["pe"].n
    DB = s["dve"].n

    def poolb(g):
        for i in range(8):
            if i >= 2:
                g.wait_ge(s["pe"].h, PB + i - 1)
            g.dma_start(out=S["wqb"][:, i % 2], in_=d["wqk"][8 + i]).then_inc(QW[i % 2].h, 16)
            QW[i % 2].inc(16)
        g.dma_start(out=S["wvb"], in_=d["wv"][0]).then_inc(s["WV"].h, 16)
        s["WV"].inc(16)
        g.wait_ge(KS[0].h, KSB[0] + 16 * 4)
        g.wait_ge(KS[1].h, KSB[1] + 16 * 4)
        g.collective_compute("AllGather", OP.bypass,
                             replica_groups=[[0, 1, 2, 3], [4, 5, 6, 7]],
                             ins=[d["kin"].opt()], outs=[d["kga"].opt()]).then_inc(s["cck"].h)
        s["cck"].inc()
        for i in range(2):
            g.wait_ge(s["pe"].h, PB + 7 + i)
            g.dma_start(out=S["wqb"][:, i % 2], in_=d["wqk"][i]).then_inc(QW[i % 2].h, 16)
            QW[i % 2].inc(16)
        g.wait_ge(s["pe"].h, PB + 16)
        g.dma_start(out=S["wvb"], in_=d["wv"][1]).then_inc(s["WV"].h, 16)
        s["WV"].inc(16)
        for i in range(2, 8):
            g.wait_ge(s["pe"].h, PB + 23 + i)
            g.dma_start(out=S["wqb"][:, i % 2], in_=d["wqk"][i]).then_inc(QW[i % 2].h, 16)
            QW[i % 2].inc(16)
        g.wait_ge(KS[0].h, KSB[0] + 16 * 12)
        g.wait_ge(KS[1].h, KSB[1] + 16 * 12)
        g.collective_compute("AllGather", OP.bypass,
                             replica_groups=[[0, 1, 2, 3], [4, 5, 6, 7]],
                             ins=[d["vin"].opt()], outs=[d["vga"].opt()]).then_inc(s["ccv"].h)
        s["ccv"].inc()

    def peb(t):
        for i in range(8):
            t.wait_ge(QW[i % 2].h, QWB[i % 2] + 16 * (i // 2 + 1))
            if i >= 2:
                t.wait_ge(s["dve"].h, DB + i - 1)
            base = 1024 * (i % 2)
            for nh in range(2):
                for hc in range(HC):
                    ins = t.matmul(PS[:, base + nh * 512:base + (nh + 1) * 512],
                                   S["wqb"][:, i % 2, hc],
                                   S["n_fm"][:, hc, nh * 512:(nh + 1) * 512],
                                   start=(hc == 0), stop=(hc == HC - 1))
            ins.then_inc(s["pe"].h, 1)
            s["pe"].inc()
        for vg in range(2):
            t.wait_ge(s["WV"].h, WVB + 16 * (vg + 1))
            for tc in range(8):
                k = vg * 8 + tc
                if k >= 4:
                    t.wait_ge(s["dve"].h, DB + 8 + (k - 4) + 1)
                for hc in range(HC):
                    ins = t.matmul(PS[:, 2048 + (tc % 4) * 512:2048 + (tc % 4 + 1) * 512],
                                   S["n_fm"][:, hc, tc * P:(tc + 1) * P],
                                   S["wvb"][:, hc],
                                   start=(hc == 0), stop=(hc == HC - 1))
                ins.then_inc(s["pe"].h, 1)
                s["pe"].inc()
        for i in range(8):
            t.wait_ge(QW[i % 2].h, QWB[i % 2] + 16 * (4 + i // 2 + 1))
            t.wait_ge(s["dve"].h, DB + 7 + i)
            base = 1024 * (i % 2)
            for nh in range(2):
                for hc in range(HC):
                    ins = t.matmul(PS[:, base + nh * 512:base + (nh + 1) * 512],
                                   S["wqb"][:, i % 2, hc],
                                   S["n_fm"][:, hc, nh * 512:(nh + 1) * 512],
                                   start=(hc == 0), stop=(hc == HC - 1))
            ins.then_inc(s["pe"].h, 1)
            s["pe"].inc()

    def dveb(v):
        for i in range(8):
            v.wait_ge(s["pe"].h, PB + i + 1)
            if i >= 2:
                v.wait_ge(KS[i % 2].h, KSB[i % 2] + 16 * ((i - 2) // 2 + 1))
            base = 1024 * (i % 2)
            v.tensor_scalar(out=S["kv_st"][:, i % 2], in0=PS[:, base:base + 1024],
                            scalar1=S["bk"][:, i:i + 1], scalar2=None,
                            op0=OP.add).then_inc(s["dve"].h, 1)
            s["dve"].inc()
        for k in range(16):
            vg, tc = divmod(k, 8)
            v.wait_ge(s["pe"].h, PB + 8 + k + 1)
            sl = k % 2
            if k < 2:
                v.wait_ge(KS[sl].h, KSB[sl] + 16 * 4)
            else:
                v.wait_ge(KS[sl].h, KSB[sl] + 16 * (4 + (k - 2) // 2 + 1))
            v.tensor_tensor(out=S["kv_st"][:, sl, 0:512],
                            in0=PS[:, 2048 + (tc % 4) * 512:2048 + (tc % 4 + 1) * 512],
                            in1=S["bvr"][:, vg * 512:(vg + 1) * 512],
                            op=OP.add).then_inc(s["dve"].h, 1)
            s["dve"].inc()
        for i in range(8):
            v.wait_ge(s["pe"].h, PB + 24 + i + 1)
            base = 1024 * (i % 2)
            v.tensor_scalar(out=S["q_bf"][:, i], in0=PS[:, base:base + 1024],
                            scalar1=S["bq"][:, i:i + 1], scalar2=1.0 / 32.0,
                            op0=OP.add, op1=OP.mult).then_inc(s["dve"].h, 1)
            s["dve"].inc()

    def storeb(q):
        for i in range(8):
            q.wait_ge(s["dve"].h, DB + i + 1)
            q.dma_start(out=d["kin"][i * P:(i + 1) * P, :],
                        in_=S["kv_st"][:, i % 2]).then_inc(KS[i % 2].h, 16)
            KS[i % 2].inc(16)
        for k in range(16):
            vg, tc = divmod(k, 8)
            q.wait_ge(s["dve"].h, DB + 8 + k + 1)
            q.dma_start(out=d["vin"][tc * P:(tc + 1) * P, vg * 512:(vg + 1) * 512],
                        in_=S["kv_st"][:, k % 2, 0:512]).then_inc(KS[k % 2].h, 16)
            KS[k % 2].inc(16)

    e.blk.gpsimd(poolb)
    e.blk.tensor(peb)
    e.blk.vector(dveb)
    e.blk.sync(storeb)
    e.qkv_pe_vdone = PB + 24


def _attn_core(e, nqt=8):
    S, s, PS, PSb = e.S, e.sem, e.PS, e.PSb
    d = e.d
    VBs = s["VB"]
    KLB = s["KL"].n

    def kload(q):
        q.wait_ge(s["cck"].h, 1)
        src = d["kga"].rearrange("(r dc p) t -> p dc r t", p=P, dc=DC)
        for dc in range(DC):
            q.dma_start(out=S["K_sb"][:, dc].rearrange("p (r t) -> p r t", r=4),
                        in_=src[:, dc]).then_inc(s["KL"].h, 16)
            s["KL"].inc(16)

    e.blk.sync(kload)
    PB = s["pe"].n
    DB = s["dve"].n
    AB = s["act"].n
    VBB = [x.n for x in VBs]

    POS_T = {}
    POS_A = {}
    _pos = 1
    for _k in range(4):
        _pos += 1
        POS_T[_k] = _pos
    for _j in range(16):
        _pos += 1
        POS_A[2 * _j] = _pos
        _pos += 1
        POS_A[2 * _j + 1] = _pos
        for _t in (2 * _j + 4, 2 * _j + 5):
            if _t < 32:
                _pos += 1
                POS_T[_t] = _pos

    def pe_scores(qt):
        return PB + 73 * qt + 1

    def pe_txp(qt, ktc):
        return PB + 73 * qt + POS_T[ktc]

    def pe_av(qt, ktc):
        return PB + 73 * qt + POS_A[ktc]

    def pe_otxp(qt, dc):
        return PB + 73 * qt + 65 + dc + 1

    def dve_evac(qt, j):
        return DB + 28 * qt + 2 + j + 1

    def dve_div(qt):
        return DB + 28 * qt + 20

    def dve_all(qt):
        return DB + 28 * (qt + 1)

    def act_exp(qt):
        return AB + qt + 1

    def vloads(q):
        q.wait_ge(s["ccv"].h, 1)
        for qt in range(nqt):
            for ktc in range(32):
                lin = 32 * qt + ktc
                if lin >= 4:
                    qt2, ktc2 = divmod(lin - 4, 32)
                    q.wait_ge(s["pe"].h, pe_av(qt2, ktc2))
                sl = lin % 4
                q.dma_start(out=S["vb"][:, sl],
                            in_=d["vga"][ktc * P:(ktc + 1) * P, :]).then_inc(VBs[sl].h, 16)
                VBs[sl].inc(16)

    def peb(t):
        EARLY_KB = (0, 1, 2, 3, 6, 7)

        def sc_mm(qt, kb, dc):
            return t.matmul(PS[:, kb * 512:(kb + 1) * 512],
                            S["q_bf"][:, dc, qt * P:(qt + 1) * P],
                            S["K_sb"][:, dc, kb * 512:(kb + 1) * 512],
                            start=(dc == 0), stop=(dc == DC - 1))

        for qt in range(nqt):
            if qt == 0:
                t.wait_ge(s["KL"].h, KLB + 128)
                for kb in EARLY_KB:
                    for dc in range(DC):
                        sc_mm(qt, kb, dc)
            else:
                t.wait_ge(s["act"].h, act_exp(qt - 1))
                t.wait_ge(s["dve"].h, dve_all(qt - 1))
            for kb in (4, 5):
                for dc in range(DC):
                    ins = sc_mm(qt, kb, dc)
            ins.then_inc(s["pe"].h, 1)
            s["pe"].inc()
            t.wait_ge(s["act"].h, act_exp(qt))

            def txp(ktc):
                j = ktc // 2
                bank = (2 if j % 2 == 0 else 6) + (ktc % 2)
                t.transpose(PSb[:, bank * 1024:bank * 1024 + P],
                            S["p_bf"][:, ktc * P:(ktc + 1) * P],
                            S["ident"]).then_inc(s["pe"].h, 1)
                s["pe"].inc()

            for k0 in range(4):
                txp(k0)
            for j in range(16):
                t.wait_ge(s["dve"].h, dve_evac(qt, j))
                for ktc in (2 * j, 2 * j + 1):
                    lin = 32 * qt + ktc
                    t.wait_ge(VBs[lin % 4].h, VBB[lin % 4] + 16 * (lin // 4 + 1))
                    for dh in range(2):
                        ins = t.matmul(PS[:, dh * 512:(dh + 1) * 512],
                                       S["pT"][:, 2 * (j % 2) + ktc % 2],
                                       S["vb"][:, lin % 4, dh * 512:(dh + 1) * 512],
                                       start=(ktc == 0), stop=(ktc == 31))
                    ins.then_inc(s["pe"].h, 1)
                    s["pe"].inc()
                for t2 in (2 * j + 4, 2 * j + 5):
                    if t2 < 32:
                        txp(t2)
            t.wait_ge(s["dve"].h, dve_div(qt))
            nxt = iter([(kb, dc2) for kb in EARLY_KB for dc2 in range(DC)]
                       if qt + 1 < nqt else [])
            for dc in range(DC):
                if dc >= 2:
                    t.wait_ge(s["dve"].h, DB + 28 * qt + 20 + (dc - 2) + 1)
                t.transpose(PSb[:, (4 + dc % 2) * 1024:(4 + dc % 2) * 1024 + P],
                            S["o_tm"][:, dc * P:(dc + 1) * P],
                            S["ident"]).then_inc(s["pe"].h, 1)
                s["pe"].inc()
                for _ in range(6):
                    nx = next(nxt, None)
                    if nx is not None:
                        sc_mm(qt + 1, nx[0], nx[1])

    def dveb(v):
        for qt in range(nqt):
            v.wait_ge(s["pe"].h, pe_scores(qt))
            v.reduce_max(S["rl_sb"], PS[:, 0:4096], axis=AX.X,
                         negate=True).then_inc(s["dve"].h, 1)
            s["dve"].inc()
            v.sem_inc(s["dve"].h, 1)
            s["dve"].inc()
            for j in range(16):
                v.wait_ge(s["pe"].h, pe_txp(qt, 2 * j + 1))
                b0 = 2 if j % 2 == 0 else 6
                psrc = PSb[:, b0 * 1024:(b0 + 2) * 1024].rearrange(
                    "p (b c) -> p b c", b=2)[:, :, 0:P]
                v.tensor_copy(out=S["pT"][:, 2 * (j % 2):2 * (j % 2) + 2, :],
                              in_=psrc).then_inc(s["dve"].h, 1)
                s["dve"].inc()
            v.wait_ge(s["act"].h, act_exp(qt))
            v.reciprocal(S["rl_sb"], S["l_sb"]).then_inc(s["dve"].h, 1)
            s["dve"].inc()
            v.wait_ge(s["pe"].h, pe_av(qt, 31))
            v.tensor_scalar(out=S["o_tm"], in0=PS[:, 0:1024], scalar1=S["rl_sb"],
                            scalar2=None, op0=OP.mult).then_inc(s["dve"].h, 1)
            s["dve"].inc()
            for dc in range(DC):
                v.wait_ge(s["pe"].h, pe_otxp(qt, dc))
                v.tensor_copy(out=S["o_fm"][:, dc, qt * P:(qt + 1) * P],
                              in_=PSb[:, (4 + dc % 2) * 1024:(4 + dc % 2) * 1024 + P]
                              ).then_inc(s["dve"].h, 1)
                s["dve"].inc()

    def actb(a):
        for qt in range(nqt):
            a.wait_ge(s["dve"].h, DB + 28 * qt + 2)
            a.activation(out=S["p_bf"], in_=PS[:, 0:4096], func=AF.Exp,
                         bias=S["rl_sb"], scale=1.0,
                         accum_out=S["l_sb"]).then_inc(s["act"].h, 1)
            s["act"].inc()

    e.blk.sync(vloads)
    e.blk.tensor(peb)
    e.blk.vector(dveb)
    e.blk.scalar(actb)
    e.attn_dve_end = DB + 28 * nqt


def _outproj(e, xsrc, xdst):
    S, s, PS = e.S, e.sem, e.PS
    d = e.d
    WO, LD, XS = s["WO"], s["LD"], s["XS"]
    WOB = [x.n for x in WO]
    LB = [x.n for x in LD]
    PB = s["pe"].n
    DB = s["dve"].n

    def poolb(g):
        for hc in range(HC):
            if hc == 0:
                g.wait_ge(s["pe"].h, e.qkv_pe_vdone)
            if hc >= 4:
                g.wait_ge(s["pe"].h, PB + 2 * (hc - 4) + 2)
            g.dma_start(out=S["wob"][:, hc % 4], in_=d["wo"][hc]).then_inc(WO[hc % 4].h, 16)
            WO[hc % 4].inc(16)

    xs_lm = {}

    def syncb(q):
        def store(k):
            q.wait_ge(s["dve"].h, DB + 2 * k + 2)
            sl2 = k % 2
            q.dma_start(out=xdst[k * P:(k + 1) * P, :],
                        in_=S["uf"][:, k % 4]).then_inc(XS[sl2].h, 16)
            xs_lm[k] = XS[sl2].inc(16)

        for hc in range(HC):
            sl = hc % 4
            if hc >= 4:
                q.wait_ge(XS[(hc - 4) % 2].h, xs_lm[hc - 4])
            q.dma_start(out=S["uf"][:, sl],
                        in_=xsrc[hc * P:(hc + 1) * P, :]).then_inc(LD[sl].h, 16)
            LD[sl].inc(16)
            if hc >= 2:
                store(hc - 2)
        store(HC - 2)
        store(HC - 1)

    def peb(t):
        t.wait_ge(s["dve"].h, e.attn_dve_end)
        for hc in range(HC):
            t.wait_ge(WO[hc % 4].h, WOB[hc % 4] + 16 * (hc // 4 + 1))
            if hc >= 2:
                t.wait_ge(s["dve"].h, DB + 2 * (hc - 2) + 2)
            for qh in range(2):
                bank = (hc % 2) * 2 + qh
                for dc in range(DC):
                    ins = t.matmul(PS[:, bank * 512:(bank + 1) * 512],
                                   S["wob"][:, hc % 4, dc],
                                   S["o_fm"][:, dc, qh * 512:(qh + 1) * 512],
                                   start=(dc == 0), stop=(dc == DC - 1))
                ins.then_inc(s["pe"].h, 1)
                s["pe"].inc()

    def dveb(v):
        for hc in range(HC):
            sl = hc % 4
            v.wait_ge(LD[sl].h, LB[sl] + 16 * (hc // 4 + 1))
            for qh in range(2):
                v.wait_ge(s["pe"].h, PB + 2 * hc + qh + 1)
                bank = (hc % 2) * 2 + qh
                v.scalar_tensor_tensor(out=S["uf"][:, sl, qh * 512:(qh + 1) * 512],
                                       in0=PS[:, bank * 512:(bank + 1) * 512],
                                       scalar=S["bo"][:, hc:hc + 1],
                                       in1=S["uf"][:, sl, qh * 512:(qh + 1) * 512],
                                       op0=OP.add, op1=OP.add).then_inc(s["dve"].h, 1)
                s["dve"].inc()

    e.blk.gpsimd(poolb)
    e.blk.sync(syncb)
    e.blk.tensor(peb)
    e.blk.vector(dveb)


_CACHE = {}


def _prepare(inputs):
    w1 = np.asarray(inputs["L0_ffn_w1"], np.float32)
    w3 = np.asarray(inputs["L0_ffn_w3"], np.float32)
    w2 = np.asarray(inputs["L0_ffn_w2"], np.float32)
    s1 = np.asarray(inputs["L0_ffn_w1_scale"], np.float32)
    s3 = np.asarray(inputs["L0_ffn_w3_scale"], np.float32)
    s2 = np.asarray(inputs["L0_ffn_w2_scale"], np.float32)
    wqkv = np.asarray(inputs["L0_attn_Wqkv"], np.float32)
    bqkv = np.asarray(inputs["L0_attn_bqkv"], np.float32)
    wo = np.asarray(inputs["L0_attn_Wo"], np.float32)
    bo = np.asarray(inputs["L0_attn_bo"], np.float32)
    x0 = np.asarray(inputs["L0_attn_x"], np.float32)

    C = np.ascontiguousarray
    shared = {
        "w1t": C(w1.reshape(IC, P, HC, P).transpose(0, 3, 2, 1)),
        "w3t": C(w3.reshape(IC, P, HC, P).transpose(0, 3, 2, 1)),
        "w2t": C(w2.reshape(4, 8, P, IC // 4, 4, P).transpose(0, 3, 5, 4, 1, 2)).reshape(4, IC // 4, P, 4, 1024),
        "s1r": C(np.broadcast_to(s1.reshape(1, IC * HC), (P, IC * HC))),
        "s3r": C(np.broadcast_to(s3.reshape(1, IC * HC), (P, IC * HC))),
        "s2r": C(np.broadcast_to(s2.T.reshape(1, IC * HC), (P, IC * HC))),
        "wqkv_qk": C(wqkv[:, :2048].reshape(HC, P, 16, P).transpose(2, 1, 0, 3)),
        "wqkv_v": C(wqkv[:, 2048:].reshape(HC, P, 2, 512).transpose(2, 1, 0, 3)),
        "bq": C(bqkv[0:1024].reshape(DC, P).T),
        "bk": C(bqkv[1024:2048].reshape(DC, P).T),
        "bvr": C(np.broadcast_to(bqkv[2048:3072][None, :], (P, D))),
        "wo": C(wo.reshape(DC, P, HC, P).transpose(2, 1, 0, 3)),
        "bo": C(bo.reshape(HC, P).T),
        "ident": np.eye(P, dtype=np.float32),
    }
    in_maps = []
    for c in range(NCORES):
        b, t0 = c // 4, (c % 4) * T
        m = dict(shared)
        m["x"] = C(x0[b, t0:t0 + T, :].T)
        in_maps.append(m)
    return in_maps


def kernel(**inputs):
    in_maps = _prepare(inputs)
    if "nc" not in _CACHE:
        _CACHE["nc"] = build()
    nc = _CACHE["nc"]
    res = run_bass_kernel_spmd(nc, in_maps, core_ids=list(range(NCORES)))
    out = np.empty((2, 4096, 4096), np.float32)
    for c in range(NCORES):
        b, t0 = c // 4, (c % 4) * T
        out[b, t0:t0 + T, :] = res.results[c]["out"].T
    return out
